# version 03
# baseline (speedup 1.0000x reference)
import contextlib
import numpy as np
import ml_dtypes
import concourse.bass as bass
import concourse.mybir as mybir
from concourse.bass_utils import run_bass_kernel_spmd

F32 = mybir.dt.float32
BF16 = mybir.dt.bfloat16
AF = mybir.ActivationFunctionType
ALU = mybir.AluOpType
AX = mybir.AxisListType

T_FULL = 8192
D = 1024
H = 128
C = 128
NCH = T_FULL // C
EPS_SCAN = 1e-6
EPS_LN = 1e-5
EPS_RMS = 1e-6
EPS_DEN = 1e-6
NE = 8
ECAP = 512
FF = 2048
HEADS = 16
DH = 64


class Ctx:

    def __init__(self, nc):
        self.nc = nc
        self.es = contextlib.ExitStack()
        self.sem_t = self.es.enter_context(nc.semaphore("sem_t"))
        self.sem_v = self.es.enter_context(nc.semaphore("sem_v"))
        self.sem_s = self.es.enter_context(nc.semaphore("sem_s"))
        self.sem_ld = self.es.enter_context(nc.semaphore("sem_ld"))
        self.sem_ld2 = self.es.enter_context(nc.semaphore("sem_ld2"))
        self.sem_st = self.es.enter_context(nc.semaphore("sem_st"))
        self.t = 0
        self.v = 0
        self.s = 0
        self.ld = 0
        self.ld2 = 0
        self.st = 0

    def barrier(self, block):
        counts = [(self.sem_t, self.t), (self.sem_v, self.v),
                  (self.sem_s, self.s), (self.sem_ld, self.ld),
                  (self.sem_ld2, self.ld2), (self.sem_st, self.st)]
        for engname in ["tensor", "vector", "scalar", "gpsimd", "sync"]:
            @getattr(block, engname)
            def _(eng, counts=counts):
                for sem, val in counts:
                    eng.wait_ge(sem, val)

    def ti(self, instr):
        self.t += 1
        instr.then_inc(self.sem_t, 1)

    def vi(self, instr):
        self.v += 1
        instr.then_inc(self.sem_v, 1)

    def si(self, instr):
        self.s += 1
        instr.then_inc(self.sem_s, 1)


_ARENA = None


def sb(nc, name, shape, dtype):
    if _ARENA is None:
        return nc.alloc_sbuf_tensor(name, shape, dtype).ap()
    return _ARENA.enter_context(nc.sbuf_tensor(name, shape, dtype)).ap()


def ps(nc, name, shape, dtype):
    if _ARENA is None:
        return nc.alloc_psum_tensor(name, shape, dtype).ap()
    return _ARENA.enter_context(nc.psum_tensor(name, shape, dtype)).ap()


def build_a1(fused_nc=None, store=None):
    nc = fused_nc or bass.Bass("TRN2", target_bir_lowering=False)
    ext = "ExternalInput"
    x_d = nc.dram_tensor("x", [T_FULL, D], F32, kind=ext).ap()
    wqT_d = nc.dram_tensor("wqT", [D, H], F32, kind=ext).ap()
    wkT_d = nc.dram_tensor("wkT", [D, H], F32, kind=ext).ap()
    wvT_d = nc.dram_tensor("wvT", [D, H], F32, kind=ext).ap()
    woT_d = nc.dram_tensor("woT", [H, D], F32, kind=ext).ap()
    bq_d = nc.dram_tensor("bq_b", [128, H], F32, kind=ext).ap()
    bk_d = nc.dram_tensor("bk_b", [128, H], F32, kind=ext).ap()
    bv_d = nc.dram_tensor("bv_b", [128, H], F32, kind=ext).ap()
    bo_d = nc.dram_tensor("bo_b", [128, D], F32, kind=ext).ap()
    idf_d = nc.dram_tensor("idf", [128, 128], F32, kind=ext).ap()
    idb_d = nc.dram_tensor("idb", [128, 128], BF16, kind=ext).ap()
    msl_d = nc.dram_tensor("msl", [128, 128], F32, kind=ext).ap()
    msu_d = nc.dram_tensor("msu", [128, 128], F32, kind=ext).ap()
    mui_d = nc.dram_tensor("mui", [128, 128], F32, kind=ext).ap()
    okind = "Internal" if fused_nc else "ExternalOutput"
    xnT_d = nc.dram_tensor("xnT", [8, 128, T_FULL], BF16, kind=okind).ap()
    x1p_d = nc.dram_tensor("x1p", [T_FULL, D], F32, kind=okind).ap()
    if store is not None:
        store["xnT"] = xnT_d
        store["x1p"] = x1p_d

    cx = Ctx(nc) if store is None or "cx" not in store else store["cx"]
    if store is not None:
        store["cx"] = cx
    _st0, _ld20 = cx.st, cx.ld2
    xin = [sb(nc, "xin0", [128, D], F32), sb(nc, "xin1", [128, D], F32)]
    tmpw = sb(nc, "tmpw", [128, D], F32)
    xm = sb(nc, "xm", [128, D], F32)
    xn_bf = sb(nc, "xn_bf", [128, D], BF16)
    xnT_sb = sb(nc, "xnT_sb", [128, D], BF16)
    msum = sb(nc, "msum", [128, 1], F32)
    meanc = sb(nc, "meanc", [128, 1], F32)
    vsum = sb(nc, "vsum", [128, 1], F32)
    rstd = sb(nc, "rstd", [128, 1], F32)
    junk = sb(nc, "junk", [128, 1], F32)
    wq_sb = sb(nc, "wq_sb", [128, D], BF16)
    wk_sb = sb(nc, "wk_sb", [128, D], BF16)
    wv_sb = sb(nc, "wv_sb", [128, D], BF16)
    wo_sb = sb(nc, "wo_sb", [128, D], BF16)
    wldf = sb(nc, "wldf", [128, D], F32)
    bq_b = sb(nc, "bq_s", [128, H], F32)
    bk_b = sb(nc, "bk_s", [128, H], F32)
    bv_b = sb(nc, "bv_s", [128, H], F32)
    bo_b = sb(nc, "bo_s", [128, D], F32)
    ident_f = sb(nc, "ident_f", [128, 128], F32)
    ident_b = sb(nc, "ident_b", [128, 128], BF16)
    mask_sl = sb(nc, "mask_sl", [128, 128], F32)
    mask_su = sb(nc, "mask_su", [128, 128], F32)
    mask_ui = sb(nc, "mask_ui", [128, 128], F32)
    q_f = sb(nc, "q_f", [128, H], F32)
    k_f = sb(nc, "k_f", [128, H], F32)
    v_f = sb(nc, "v_f", [128, H], F32)
    s_col = sb(nc, "s_col", [128, 1], F32)
    kn = sb(nc, "kn", [128, 1], F32)
    uscale = sb(nc, "uscale", [128, 1], F32)
    u_tok = sb(nc, "u_tok", [128, 128], F32)
    u_tok_b = sb(nc, "u_tok_b", [128, 128], BF16)
    q_tok_b = sb(nc, "q_tok_b", [128, 128], BF16)
    v_tok_b = sb(nc, "v_tok_b", [128, 128], BF16)
    Uf = sb(nc, "Uf", [128, 128], BF16)
    Qf = sb(nc, "Qf", [128, 128], BF16)
    A_sb = sb(nc, "A_sb", [128, 128], BF16)
    MT_sb = sb(nc, "MT_sb", [128, 128], BF16)
    A_f32 = sb(nc, "A_f32", [128, 128], F32)
    MT_f32 = sb(nc, "MT_f32", [128, 128], F32)
    W_sb = sb(nc, "W_sb", [128, 128], BF16)
    WT_sb = sb(nc, "WT_sb", [128, 128], BF16)
    S_T = sb(nc, "S_T", [128, 128], F32)
    Cp = sb(nc, "Cp", [128, 128], F32)
    RD = sb(nc, "RD", [128, 128], F32)
    rd_col = sb(nc, "rd_col", [128, 1], F32)
    srd_col = sb(nc, "srd_col", [128, 1], F32)
    CT = sb(nc, "CT", [128, 128], F32)
    N_bf = sb(nc, "N_bf", [128, 128], BF16)
    NT_bf = sb(nc, "NT_bf", [128, 128], BF16)
    P_bf = sb(nc, "P_bf", [128, 128], BF16)
    PT_bf = sb(nc, "PT_bf", [128, 128], BF16)
    Xf_bf = sb(nc, "Xf_bf", [128, 128], BF16)
    Xr_f = sb(nc, "Xr_f", [128, 128], F32)
    Xr_bf = sb(nc, "Xr_bf", [128, 128], BF16)
    Zneg_bf = sb(nc, "Zneg_bf", [128, 128], BF16)
    Alpha_bf = sb(nc, "Alpha_bf", [128, 128], BF16)
    coef_bf = sb(nc, "coef_bf", [128, 128], BF16)
    O_bf = sb(nc, "O_bf", [128, 128], BF16)
    Of_bf = sb(nc, "Of_bf", [128, 128], BF16)
    x1p_sb = [sb(nc, "x1p0", [128, D], F32), sb(nc, "x1p1", [128, D], F32)]
    tmp128 = sb(nc, "tmp128", [128, 128], F32)

    P_row0 = ps(nc, "P_row0", [128, 512], F32)
    P_row1 = ps(nc, "P_row1", [128, 512], F32)
    P_a = ps(nc, "P_a", [128, 512], F32)
    P_b = ps(nc, "P_b", [128, 512], F32)
    P_c = ps(nc, "P_c", [128, 512], F32)
    P_bfp = ps(nc, "P_bfp", [128, 1024], BF16)
    P_wide = ps(nc, "P_wide", [128, 1024], F32)
    rows = [P_row0[:, 0:128], P_row1[:, 0:128]]
    Pa = P_a[:, 0:128]
    Pb = P_b[:, 0:128]
    Pc = P_c[:, 0:128]

    with nc.Block() as block:
        if fused_nc is not None:
            cx.barrier(block)
        @block.sync
        def _(sync):
            for (dst, src) in [(ident_f, idf_d), (ident_b, idb_d),
                               (mask_sl, msl_d), (mask_su, msu_d),
                               (mask_ui, mui_d), (bq_b, bq_d), (bk_b, bk_d),
                               (bv_b, bv_d), (bo_b, bo_d)]:
                sync.dma_start(out=dst[:], in_=src).then_inc(cx.sem_ld, 16)
                cx.ld += 16

        for wsb, wd, dd in [(wq_sb, wqT_d, H), (wk_sb, wkT_d, H),
                            (wv_sb, wvT_d, H), (wo_sb, woT_d, D)]:
            @block.sync
            def _(sync, wd=wd, dd=dd):
                sync.wait_ge(cx.sem_v, cx.v)
                if dd == H:
                    sync.dma_start(
                        out=wldf.rearrange("p (s h) -> p s h", s=8),
                        in_=wd.rearrange("(s p) h -> p s h", p=128),
                    ).then_inc(cx.sem_ld, 16)
                else:
                    sync.dma_start(out=wldf[:], in_=wd).then_inc(cx.sem_ld, 16)
                cx.ld += 16

            @block.vector
            def _(vector, wsb=wsb):
                vector.wait_ge(cx.sem_ld, cx.ld)
                cx.vi(vector.tensor_copy(wsb[:], wldf[:]))

        @block.vector
        def _(vector):
            vector.wait_ge(cx.sem_ld, cx.ld)
            vector.memset(MT_sb[:], 0.0)
            vector.memset(MT_f32[:], 0.0)
            vector.tensor_copy(A_f32[:], ident_f[:])
            cx.vi(vector.tensor_copy(A_sb[:], ident_b[:]))

        @block.sync
        def _(sync):
            sync.dma_start(out=xin[0][:], in_=x_d[0:128, :]).then_inc(cx.sem_ld2, 16)
            cx.ld2 += 16

        for c in range(NCH):
            buf = c % 2
            x_sb = xin[buf]

            if c + 1 < NCH:
                @block.sync
                def _(sync, c=c, buf=buf):
                    if c >= 1:
                        sync.wait_ge(cx.sem_st, _st0 + 32 * c)
                    sync.dma_start(
                        out=xin[1 - buf][:],
                        in_=x_d[(c + 1) * 128:(c + 2) * 128, :],
                    ).then_inc(cx.sem_ld2, 16)
                    cx.ld2 += 16

            @block.vector
            def _(vector, c=c, x_sb=x_sb):
                vector.wait_ge(cx.sem_ld2, _ld20 + 16 * (c + 1))
                if c > 0:
                    vector.wait_ge(cx.sem_t, cx.t)
                vector.reduce_sum(msum[:], x_sb[:], axis=AX.X)
                vector.tensor_copy(junk[:], msum[:])
                vector.tensor_scalar_mul(meanc[:], msum[:], 1.0 / D)
                vector.tensor_copy(junk[:], meanc[:])
                vector.tensor_scalar(xm[:], x_sb[:], meanc[:], None, ALU.subtract)
                vector.tensor_tensor(tmpw[:], xm[:], xm[:], ALU.mult)
                vector.reduce_sum(vsum[:], tmpw[:], axis=AX.X)
                vector.tensor_copy(junk[:], meanc[:])
                cx.vi(vector.tensor_scalar(rstd[:], vsum[:], 1.0 / D, EPS_LN,
                                           ALU.mult, ALU.add))

            @block.scalar
            def _(scalar):
                scalar.wait_ge(cx.sem_v, cx.v)
                cx.si(scalar.sqrt(rstd[:], rstd[:]))

            @block.vector
            def _(vector, c=c):
                vector.wait_ge(cx.sem_s, cx.s)
                vector.engine_nop()
                vector.reciprocal(rstd[:], rstd[:])
                vector.tensor_copy(junk[:], meanc[:])
                vector.tensor_scalar_mul(tmpw[:], xm[:], rstd[:])
                cx.vi(vector.tensor_copy(xn_bf[:], tmpw[:]))

            @block.tensor
            def _(tensor, c=c):
                tensor.wait_ge(cx.sem_v, cx.v)
                for ds in range(8):
                    cx.ti(tensor.transpose(
                        P_bfp[:, ds * 128:(ds + 1) * 128],
                        xn_bf[:, ds * 128:(ds + 1) * 128], ident_b[:]))

            @block.vector
            def _(vector, c=c):
                vector.wait_ge(cx.sem_t, cx.t)
                if c >= 2:
                    vector.wait_ge(cx.sem_st, _st0 + 32 * (c - 1) + 16)
                cx.vi(vector.tensor_copy(xnT_sb[:], P_bfp[:, 0:1024]))

            @block.gpsimd
            def _(gpsimd, c=c):
                gpsimd.wait_ge(cx.sem_v, cx.v)
                gpsimd.dma_start(
                    out=xnT_d[:, :, c * 128:(c + 1) * 128].rearrange(
                        "s p t -> p s t"),
                    in_=xnT_sb.rearrange("p (s t) -> p s t", s=8),
                ).then_inc(cx.sem_st, 16)
                cx.st += 16

            @block.tensor
            def _(tensor, c=c):
                tensor.wait_ge(cx.sem_v, cx.v)
                for (dstp, w) in [(Pa, wq_sb), (Pb, wk_sb), (Pc, wv_sb)]:
                    for ds in range(8):
                        cx.ti(tensor.matmul(
                            dstp, xnT_sb[:, ds * 128:(ds + 1) * 128],
                            w[:, ds * 128:(ds + 1) * 128],
                            start=(ds == 0), stop=(ds == 7)))

            @block.vector
            def _(vector, c=c):
                vector.wait_ge(cx.sem_t, cx.t)
                vector.tensor_tensor(q_f[:], Pa, bq_b[:], ALU.add)
                vector.tensor_tensor(k_f[:], Pb, bk_b[:], ALU.add)
                vector.tensor_tensor(v_f[:], Pc, bv_b[:], ALU.add)
                vector.tensor_tensor(tmp128[:], q_f[:], k_f[:], ALU.mult)
                vector.reduce_sum(s_col[:], tmp128[:], axis=AX.X)
                vector.tensor_tensor(tmp128[:], k_f[:], k_f[:], ALU.mult)
                cx.vi(vector.reduce_sum(kn[:], tmp128[:], axis=AX.X))

            @block.scalar
            def _(scalar):
                scalar.wait_ge(cx.sem_v, cx.v)
                cx.si(scalar.sqrt(uscale[:], kn[:]))

            @block.vector
            def _(vector, c=c):
                vector.wait_ge(cx.sem_s, cx.s)
                vector.tensor_scalar_add(uscale[:], uscale[:], EPS_SCAN)
                vector.engine_nop()
                vector.reciprocal(uscale[:], uscale[:])
                vector.tensor_copy(q_tok_b[:], q_f[:])
                vector.tensor_copy(v_tok_b[:], v_f[:])
                vector.tensor_scalar_mul(u_tok[:], k_f[:], uscale[:])
                cx.vi(vector.tensor_copy(u_tok_b[:], u_tok[:]))

            emit_scan_chunk(
                nc, block, cx, c,
                dict(u_tok_b=u_tok_b, q_tok_b=q_tok_b, v_tok_b=v_tok_b,
                     Uf=Uf, Qf=Qf, A_sb=A_sb, MT_sb=MT_sb, A_f32=A_f32,
                     MT_f32=MT_f32, W_sb=W_sb, WT_sb=WT_sb, S_T=S_T, Cp=Cp,
                     RD=RD, rd_col=rd_col, srd_col=srd_col, CT=CT, N_bf=N_bf,
                     NT_bf=NT_bf, P_bf=P_bf, PT_bf=PT_bf, Xf_bf=Xf_bf,
                     Xr_f=Xr_f, Xr_bf=Xr_bf, Zneg_bf=Zneg_bf,
                     Alpha_bf=Alpha_bf, coef_bf=coef_bf, O_bf=O_bf,
                     s_col=s_col, tmp128=tmp128, junk=junk,
                     ident_f=ident_f, ident_b=ident_b, mask_sl=mask_sl,
                     mask_su=mask_su, mask_ui=mask_ui),
                rows, Pa, Pb, Pc, P_bfp, last=(c == NCH - 1))

            @block.tensor
            def _(tensor, c=c):
                tensor.wait_ge(cx.sem_v, cx.v)
                cx.ti(tensor.transpose(P_bfp[:, 0:128], O_bf[:], ident_b[:]))

            @block.vector
            def _(vector, c=c):
                vector.wait_ge(cx.sem_t, cx.t)
                cx.vi(vector.tensor_copy(Of_bf[:], P_bfp[:, 0:128]))

            @block.tensor
            def _(tensor, c=c):
                tensor.wait_ge(cx.sem_v, cx.v)
                for half in range(2):
                    cx.ti(tensor.matmul(
                        P_wide[:, half * 512:(half + 1) * 512], Of_bf[:],
                        wo_sb[:, half * 512:(half + 1) * 512],
                        start=True, stop=True))

            @block.vector
            def _(vector, c=c, x_sb=x_sb, buf=buf):
                vector.wait_ge(cx.sem_t, cx.t)
                if c >= 2:
                    vector.wait_ge(cx.sem_st, _st0 + 32 * (c - 1) + 32)
                vector.tensor_tensor(tmpw[:], P_wide[:, 0:1024], bo_b[:], ALU.add)
                cx.vi(vector.tensor_tensor(x1p_sb[buf][:], tmpw[:], x_sb[:], ALU.add))

            @block.gpsimd
            def _(gpsimd, c=c, buf=buf):
                gpsimd.wait_ge(cx.sem_v, cx.v)
                gpsimd.dma_start(
                    out=x1p_d[c * 128:(c + 1) * 128, :],
                    in_=x1p_sb[buf][:]).then_inc(cx.sem_st, 16)
                cx.st += 16

        @block.gpsimd
        def _(gpsimd):
            gpsimd.wait_ge(cx.sem_st, cx.st)
    return nc


def emit_scan_chunk(nc, block, cx, c, tn, rows, Pa, Pb, Pc, P_bfp, last):
    t = tn

    @block.tensor
    def _(tensor, c=c):
        tensor.wait_ge(cx.sem_v, cx.v)
        cx.ti(tensor.transpose(P_bfp[:, 0:128], t["u_tok_b"][:], tn["ident_b"][:]))
        cx.ti(tensor.transpose(P_bfp[:, 128:256], t["q_tok_b"][:], tn["ident_b"][:]))

    @block.vector
    def _(vector, c=c):
        vector.wait_ge(cx.sem_t, cx.t)
        vector.tensor_copy(t["Uf"][:], P_bfp[:, 0:128])
        cx.vi(vector.tensor_copy(t["Qf"][:], P_bfp[:, 128:256]))

    @block.tensor
    def _(tensor, c=c):
        tensor.wait_ge(cx.sem_v, cx.v)
        cx.ti(tensor.matmul(Pa, t["A_sb"][:], t["Uf"][:], start=True, stop=True))

    @block.vector
    def _(vector, c=c):
        vector.wait_ge(cx.sem_t, cx.t)
        cx.vi(vector.tensor_copy(t["W_sb"][:], Pa))

    @block.tensor
    def _(tensor, c=c):
        tensor.wait_ge(cx.sem_v, cx.v)
        cx.ti(tensor.matmul(Pb, t["Uf"][:], t["W_sb"][:], start=True, stop=True))
        cx.ti(tensor.transpose(P_bfp[:, 0:128], t["W_sb"][:], tn["ident_b"][:]))

    @block.vector
    def _(vector, c=c):
        vector.wait_ge(cx.sem_t, cx.t)
        vector.tensor_copy(t["WT_sb"][:], P_bfp[:, 0:128])
        cx.vi(vector.tensor_tensor(t["S_T"][:], Pb, tn["ident_f"][:], ALU.add))

    v_base = cx.v

    @block.tensor
    def _(tensor, c=c, v_base=v_base):
        for j in range(C):
            tensor.wait_ge(cx.sem_v, v_base + j)
            cx.ti(tensor.matmul(
                rows[j % 2][:, 0:C - j],
                tn["ident_f"][:, j:j + 1].to_broadcast([128, 128]),
                t["S_T"][:, j:C], start=True, stop=True))

    t_base = cx.t - C

    @block.vector
    def _(vector, c=c, t_base=t_base):
        for j in range(C):
            rep = rows[j % 2]
            vector.wait_ge(cx.sem_t, t_base + j + 1)
            vector.reciprocal(t["RD"][:, j:j + 1], rep[:, 0:1])
            vector.engine_nop()
            vector.scalar_tensor_tensor(
                t["Cp"][:, j:j + 1], t["S_T"][:, j:j + 1], -1.0,
                t["RD"][:, j:j + 1], ALU.mult, ALU.mult)
            vector.tensor_copy(t["junk"][:], t["s_col"][:])
            if j < C - 1:
                instr = vector.scalar_tensor_tensor(
                    t["S_T"][:, j + 1:C], rep[:, 1:C - j], t["Cp"][:, j:j + 1],
                    t["S_T"][:, j + 1:C], ALU.mult, ALU.add)
            else:
                instr = vector.engine_nop()
            cx.vi(instr)

    @block.vector
    def _(vector, c=c):
        vector.tensor_tensor(t["tmp128"][:], t["RD"][:], tn["ident_f"][:], ALU.mult)
        vector.reduce_sum(t["rd_col"][:], t["tmp128"][:], axis=AX.X)
        vector.scalar_tensor_tensor(
            t["NT_bf"][:], t["Cp"][:], -1.0, tn["mask_sl"][:], ALU.mult, ALU.mult)
        cx.vi(vector.tensor_tensor(t["srd_col"][:], t["s_col"][:], t["rd_col"][:], ALU.mult))

    @block.tensor
    def _(tensor, c=c):
        tensor.wait_ge(cx.sem_v, cx.v)
        cx.ti(tensor.transpose(Pa[0:64, 0:128], t["Cp"][:, 0:64], tn["ident_f"][:]))
        cx.ti(tensor.transpose(Pb[0:64, 0:128], t["Cp"][:, 64:128], tn["ident_f"][:]))

    @block.vector
    def _(vector, c=c):
        vector.wait_ge(cx.sem_t, cx.t)
        vector.tensor_copy(t["CT"][0:64, :], Pa[0:64, 0:128])
        vector.tensor_copy(t["CT"][64:128, :], Pb[0:64, 0:128])
        vector.scalar_tensor_tensor(
            t["N_bf"][:], t["CT"][:], -1.0, tn["mask_su"][:], ALU.mult, ALU.mult)
        vector.tensor_tensor(t["tmp128"][:], t["CT"][:], tn["mask_su"][:], ALU.mult)
        vector.tensor_tensor(t["P_bf"][:], t["tmp128"][:], tn["ident_f"][:], ALU.add)
        vector.tensor_tensor(t["tmp128"][:], t["Cp"][:], tn["mask_sl"][:], ALU.mult)
        cx.vi(vector.tensor_tensor(t["PT_bf"][:], t["tmp128"][:], tn["ident_f"][:], ALU.add))

    for lvl in range(6):
        @block.tensor
        def _(tensor, c=c, lvl=lvl):
            tensor.wait_ge(cx.sem_v, cx.v)
            cx.ti(tensor.matmul(Pa, t["NT_bf"][:], t["N_bf"][:], start=True, stop=True))
            cx.ti(tensor.matmul(Pb, t["N_bf"][:], t["NT_bf"][:], start=True, stop=True))

        @block.vector
        def _(vector, c=c, lvl=lvl):
            vector.wait_ge(cx.sem_t, cx.t)
            vector.tensor_copy(t["N_bf"][:], Pa)
            cx.vi(vector.tensor_copy(t["NT_bf"][:], Pb))

        @block.tensor
        def _(tensor, c=c, lvl=lvl):
            tensor.wait_ge(cx.sem_v, cx.v)
            cx.ti(tensor.matmul(Pa, t["PT_bf"][:], t["N_bf"][:], start=True, stop=True))
            if lvl < 5:
                cx.ti(tensor.matmul(Pb, t["N_bf"][:], t["PT_bf"][:], start=True, stop=True))

        @block.vector
        def _(vector, c=c, lvl=lvl):
            vector.wait_ge(cx.sem_t, cx.t)
            vector.tensor_tensor(t["P_bf"][:], t["P_bf"][:], Pa, ALU.add)
            if lvl < 5:
                vector.tensor_tensor(t["PT_bf"][:], t["PT_bf"][:], Pb, ALU.add)
            cx.vi(vector.engine_nop())

    @block.tensor
    def _(tensor, c=c):
        tensor.wait_ge(cx.sem_v, cx.v)
        cx.ti(tensor.matmul(Pa, t["WT_sb"][:], t["P_bf"][:], start=True, stop=True))

    @block.vector
    def _(vector, c=c):
        vector.wait_ge(cx.sem_t, cx.t)
        cx.vi(vector.tensor_copy(t["Xf_bf"][:], Pa))

    @block.tensor
    def _(tensor, c=c):
        tensor.wait_ge(cx.sem_v, cx.v)
        cx.ti(tensor.transpose(P_bfp[:, 0:128], t["Xf_bf"][:], tn["ident_b"][:]))
        cx.ti(tensor.matmul(Pc, t["Xf_bf"][:], t["Qf"][:], start=True, stop=True))

    @block.vector
    def _(vector, c=c):
        vector.wait_ge(cx.sem_t, cx.t)
        vector.tensor_copy(t["Xr_f"][:], P_bfp[:, 0:128])
        vector.tensor_copy(t["Xr_bf"][:], P_bfp[:, 0:128])
        vector.tensor_scalar(t["Zneg_bf"][:], t["Xr_f"][:], t["rd_col"][:], -1.0,
                             ALU.mult, ALU.mult)
        vector.tensor_scalar_mul(t["Alpha_bf"][:], t["Xr_f"][:], t["srd_col"][:])
        cx.vi(vector.scalar_tensor_tensor(
            t["coef_bf"][:], Pc, t["srd_col"][:], tn["mask_ui"][:],
            ALU.mult, ALU.mult))

    @block.tensor
    def _(tensor, c=c):
        tensor.wait_ge(cx.sem_v, cx.v)
        cx.ti(tensor.matmul(Pa, t["Qf"][:], t["MT_sb"][:], start=True, stop=False))
        cx.ti(tensor.matmul(Pa, t["coef_bf"][:], t["v_tok_b"][:], start=False, stop=True))
        cx.ti(tensor.matmul(Pb, t["Alpha_bf"][:], t["v_tok_b"][:], start=True, stop=True))
        cx.ti(tensor.matmul(Pc, t["Zneg_bf"][:], t["Xr_bf"][:], start=True, stop=True))

    @block.vector
    def _(vector, c=c):
        vector.wait_ge(cx.sem_t, cx.t)
        vector.tensor_copy(t["O_bf"][:], Pa)
        vector.tensor_tensor(t["MT_f32"][:], t["MT_f32"][:], Pb, ALU.add)
        vector.tensor_tensor(t["A_f32"][:], t["A_f32"][:], Pc, ALU.add)
        vector.tensor_copy(t["MT_sb"][:], t["MT_f32"][:])
        cx.vi(vector.tensor_copy(t["A_sb"][:], t["A_f32"][:]))



TB0 = 1024
NB0 = T_FULL // TB0


def build_a2(fused_nc=None, store=None, debug=False):
    nc = fused_nc or bass.Bass("TRN2", target_bir_lowering=False)
    ext = "ExternalInput"
    if debug:
        wdump_d = nc.dram_tensor("wdump", [NB0, 128, 8 * NE], F32,
                                 kind="ExternalOutput").ap()
        gdump_d = nc.dram_tensor("gdump", [NB0, 128, 8 * (3 * NE + 3)], F32,
                                 kind="ExternalOutput").ap()
    if store is None:
        xnT_d = nc.dram_tensor("xnT", [8, 128, T_FULL], BF16, kind=ext).ap()
        x1p_d = nc.dram_tensor("x1p", [T_FULL, D], F32, kind=ext).ap()
    else:
        xnT_d = store["xnT"]
        x1p_d = store["x1p"]
    gw_d = nc.dram_tensor("gw", [D, NE], F32, kind=ext).ap()
    w1_d = nc.dram_tensor("w1", [NE, D, ECAP], F32, kind=ext).ap()
    b1_d = nc.dram_tensor("b1l", [128, NE * 4], F32, kind=ext).ap()
    w2_d = nc.dram_tensor("w2", [NE, ECAP, D], F32, kind=ext).ap()
    b2_d = nc.dram_tensor("b2l", [NE, D], F32, kind=ext).ap()
    idb_d = nc.dram_tensor("idb2", [128, 128], BF16, kind=ext).ap()
    okind = "Internal" if fused_nc else "ExternalOutput"
    x1_d = nc.dram_tensor("x1", [T_FULL, D], F32, kind=okind).ap()
    if store is not None:
        store["x1"] = x1_d

    cx = Ctx(nc) if fused_nc is None else store["cx"]
    _st0, _ld20 = cx.st, cx.ld2
    NT = 8

    w1_sb = sb(nc, "a2_w1", [128, NE * 8 * ECAP], BF16)
    w2_sb = sb(nc, "a2_w2", [128, NE * 4 * D], BF16)
    gw_sb = sb(nc, "a2_gw", [128, 8 * NE], BF16)
    b1_sb = sb(nc, "a2_b1", [128, NE * 4], F32)
    b2_sb = sb(nc, "a2_b2", [NE, D], BF16)
    ident_b = sb(nc, "a2_idb", [128, 128], BF16)
    xnT_blk = sb(nc, "a2_xnT", [128, 8 * TB0], BF16)
    wld = xnT_blk.bitcast(F32)
    acc = sb(nc, "a2_acc", [128, 8 * D], F32)
    h_sb = sb(nc, "a2_h", [128, 4 * TB0], BF16)
    gl_all = sb(nc, "a2_glall", [128, NT * NE], F32)
    ex_all = sb(nc, "a2_exall", [128, NT * NE], F32)
    msk_all = sb(nc, "a2_mskall", [128, NT * NE], F32)
    mskd_all = sb(nc, "a2_mskdall", [128, NT * NE], F32)
    selt_all = sb(nc, "a2_seltall", [128, NT * NE], F32)
    sel2 = sb(nc, "a2_sel2", [128, NT * NE], F32)
    m1L = sb(nc, "a2_m1L", [128, NT], F32)
    m2L = sb(nc, "a2_m2L", [128, NT], F32)
    ssum8 = sb(nc, "a2_ssum8", [128, NT], F32)
    num8 = sb(nc, "a2_num8", [128, NT], F32)
    rs8 = sb(nc, "a2_rs8", [128, NT], F32)
    tokw8 = sb(nc, "a2_tokw8", [128, NT], F32)
    wsel = sb(nc, "a2_wsel", [128, NT * NE], F32)
    if debug:
        gdbg = sb(nc, "a2_gdbg", [128, 8 * (3 * NE + 3)], F32)
    wsel_bf_all = sb(nc, "a2_wselb", [128, NT * NE], BF16)
    wselT = sb(nc, "a2_wselT", [NE, 128], BF16)
    junk = sb(nc, "a2_junk", [128, 1], F32)
    junk2 = sb(nc, "a2_junk2", [128, 1], F32)

    P_h = ps(nc, "a2_Ph", [128, 512], F32)
    P_y = ps(nc, "a2_Py", [128, 512], F32)
    P_g = ps(nc, "a2_Pg", [128, 512], F32)
    P_t = ps(nc, "a2_Pt", [128, 1024], F32)
    P_bf2 = ps(nc, "a2_Pbf", [128, 1024], BF16)

    with nc.Block() as block:
        if fused_nc is not None:
            cx.barrier(block)
        @block.sync
        def _(sync):
            sync.dma_start(out=ident_b[:], in_=idb_d).then_inc(cx.sem_ld, 16)
            sync.dma_start(out=b1_sb[:], in_=b1_d).then_inc(cx.sem_ld, 16)
            cx.ld += 32

        @block.sync
        def _(sync):
            sync.dma_start(
                out=wld[:, 0:64].rearrange("p (s g) -> p s g", s=8),
                in_=gw_d.rearrange("(s p) g -> p s g", p=128),
            ).then_inc(cx.sem_ld, 16)
            cx.ld += 16

        @block.vector
        def _(vector):
            vector.wait_ge(cx.sem_ld, cx.ld)
            cx.vi(vector.tensor_copy(gw_sb[:], wld[:, 0:64]))

        @block.sync
        def _(sync):
            sync.wait_ge(cx.sem_v, cx.v)
            sync.dma_start(out=wld[0:NE, 0:D], in_=b2_d).then_inc(cx.sem_ld, 16)
            cx.ld += 16

        @block.vector
        def _(vector):
            vector.wait_ge(cx.sem_ld, cx.ld)
            cx.vi(vector.tensor_copy(b2_sb[:], wld[0:NE, 0:D]))

        for e in range(NE):
            @block.sync
            def _(sync, e=e):
                sync.wait_ge(cx.sem_v, cx.v)
                sync.dma_start(
                    out=wld.rearrange("p (s f) -> p s f", s=8),
                    in_=w1_d[e].rearrange("(s p) f -> p s f", p=128),
                ).then_inc(cx.sem_ld, 16)
                cx.ld += 16

            @block.vector
            def _(vector, e=e):
                vector.wait_ge(cx.sem_ld, cx.ld)
                cx.vi(vector.tensor_copy(w1_sb[:, e * 4096:(e + 1) * 4096], wld[:]))

        for e in range(NE):
            @block.sync
            def _(sync, e=e):
                sync.wait_ge(cx.sem_v, cx.v)
                sync.dma_start(
                    out=wld.rearrange("p (s d) -> p s d", s=4),
                    in_=w2_d[e].rearrange("(s p) d -> p s d", p=128),
                ).then_inc(cx.sem_ld, 16)
                cx.ld += 16

            @block.vector
            def _(vector, e=e):
                vector.wait_ge(cx.sem_ld, cx.ld)
                cx.vi(vector.tensor_copy(w2_sb[:, e * 4096:(e + 1) * 4096], wld[:]))

        st_per_blk = 16
        for b in range(NB0):
            @block.sync
            def _(sync, b=b):
                if b > 0:
                    sync.wait_ge(cx.sem_st, cx.st)
                else:
                    sync.wait_ge(cx.sem_v, cx.v)
                sync.dma_start(
                    out=xnT_blk.rearrange("p (s t) -> p s t", s=8),
                    in_=xnT_d[:, :, b * TB0:(b + 1) * TB0].rearrange("s p t -> p s t"),
                ).then_inc(cx.sem_ld2, 16)
                sync.dma_start(
                    out=acc.rearrange("p (tt d) -> p tt d", tt=8),
                    in_=x1p_d[b * TB0:(b + 1) * TB0, :].rearrange("(tt p) d -> p tt d", p=128),
                ).then_inc(cx.sem_ld2, 16)
                cx.ld2 += 32

            @block.tensor
            def _(tensor, b=b):
                tensor.wait_ge(cx.sem_ld2, cx.ld2)
                tensor.wait_ge(cx.sem_v, cx.v)
                for tt in range(NT):
                    for ds in range(8):
                        cx.ti(tensor.matmul(
                            P_g[:, tt * NE:(tt + 1) * NE],
                            xnT_blk[:, ds * TB0 + tt * 128: ds * TB0 + (tt + 1) * 128],
                            gw_sb[:, ds * NE:(ds + 1) * NE],
                            start=(ds == 0), stop=(ds == 7)))

            @block.vector
            def _(vector, b=b):
                vector.wait_ge(cx.sem_t, cx.t)
                vector.tensor_copy(gl_all[:], P_g[:, 0:NT * NE])
                vector.tensor_copy(junk[:], junk2[:])
                vector.tensor_copy(junk2[:], junk[:])
                cx.vi(vector.tensor_copy(junk[:], junk2[:]))

            @block.scalar
            def _(scalar, b=b):
                scalar.wait_ge(cx.sem_v, cx.v)
                cx.si(scalar.activation(ex_all[:], gl_all[:], AF.Exp))

            @block.vector
            def _(vector, b=b):
                vector.wait_ge(cx.sem_s, cx.s)
                for tt in range(NT):
                    vector.reduce_max(m1L[:, tt:tt + 1],
                                      gl_all[:, tt * NE:(tt + 1) * NE], axis=AX.X)
                for tt in range(NT):
                    vector.tensor_scalar(msk_all[:, tt * NE:(tt + 1) * NE],
                                         gl_all[:, tt * NE:(tt + 1) * NE],
                                         m1L[:, tt:tt + 1], None, ALU.is_equal)
                for tt in range(NT):
                    vector.scalar_tensor_tensor(
                        mskd_all[:, tt * NE:(tt + 1) * NE],
                        msk_all[:, tt * NE:(tt + 1) * NE], -1e30,
                        gl_all[:, tt * NE:(tt + 1) * NE], ALU.mult, ALU.add)
                for tt in range(NT):
                    vector.reduce_max(m2L[:, tt:tt + 1],
                                      mskd_all[:, tt * NE:(tt + 1) * NE], axis=AX.X)
                for tt in range(NT):
                    vector.tensor_scalar(selt_all[:, tt * NE:(tt + 1) * NE],
                                         gl_all[:, tt * NE:(tt + 1) * NE],
                                         m2L[:, tt:tt + 1], None, ALU.is_ge)
                for tt in range(NT):
                    vector.reduce_sum(ssum8[:, tt:tt + 1],
                                      ex_all[:, tt * NE:(tt + 1) * NE], axis=AX.X)
                vector.tensor_tensor(sel2[:], selt_all[:], ex_all[:], ALU.mult)
                vector.tensor_copy(junk[:], junk2[:])
                vector.tensor_copy(junk2[:], junk[:])
                vector.reciprocal(rs8[:], ssum8[:])
                for tt in range(NT):
                    vector.reduce_sum(num8[:, tt:tt + 1],
                                      sel2[:, tt * NE:(tt + 1) * NE], axis=AX.X)
                vector.tensor_copy(junk[:], junk2[:])
                vector.tensor_copy(junk2[:], junk[:])
                vector.tensor_copy(junk[:], junk2[:])
                vector.tensor_tensor(tokw8[:], num8[:], rs8[:], ALU.mult)
                vector.tensor_copy(junk[:], junk2[:])
                vector.tensor_copy(junk2[:], junk[:])
                vector.tensor_copy(junk[:], junk2[:])
                for tt in range(NT):
                    vector.tensor_scalar_mul(wsel[:, tt * NE:(tt + 1) * NE],
                                             selt_all[:, tt * NE:(tt + 1) * NE],
                                             tokw8[:, tt:tt + 1])
                vector.tensor_copy(junk[:], junk2[:])
                vector.tensor_copy(junk2[:], junk[:])
                vector.tensor_copy(junk[:], junk2[:])
                vector.tensor_copy(wsel_bf_all[:], wsel[:, 0:NT * NE])
                if debug:
                    vector.tensor_copy(gdbg[:, 0:64], gl_all[:])
                    vector.tensor_copy(gdbg[:, 64:128], ex_all[:])
                    vector.tensor_copy(gdbg[:, 128:136], m1L[:])
                    vector.tensor_copy(gdbg[:, 136:144], m2L[:])
                    vector.tensor_copy(gdbg[:, 144:152], ssum8[:])
                    vector.tensor_copy(gdbg[:, 152:160], tokw8[:])
                vector.tensor_copy(junk2[:], junk[:])
                cx.vi(vector.tensor_copy(junk[:], junk2[:]))

            for tt in range(NT):
                @block.tensor
                def _(tensor, b=b, tt=tt):
                    tensor.wait_ge(cx.sem_v, cx.v)
                    cx.ti(tensor.transpose(
                        P_bf2[:, 0:128][0:NE, :],
                        wsel_bf_all[:, tt * NE:(tt + 1) * NE], ident_b[:]))

                @block.vector
                def _(vector, b=b, tt=tt):
                    vector.wait_ge(cx.sem_t, cx.t)
                    cx.vi(vector.tensor_copy(wselT[:], P_bf2[0:NE, 0:128]))

                @block.tensor
                def _(tensor, b=b, tt=tt):
                    tensor.wait_ge(cx.sem_v, cx.v)
                    for half in range(2):
                        cx.ti(tensor.matmul(
                            P_t[:, half * 512:(half + 1) * 512], wselT[:],
                            b2_sb[:, half * 512:(half + 1) * 512],
                            start=True, stop=True))

                @block.vector
                def _(vector, b=b, tt=tt):
                    vector.wait_ge(cx.sem_t, cx.t)
                    cx.vi(vector.tensor_tensor(
                        acc[:, tt * D:(tt + 1) * D], acc[:, tt * D:(tt + 1) * D],
                        P_t[:, 0:1024], ALU.add))

            for e in range(NE):
                for ft in range(4):
                    for tc in range(TB0 // 512):
                        @block.tensor
                        def _(tensor, b=b, e=e, ft=ft, tc=tc):
                            tensor.wait_ge(cx.sem_v, cx.v)
                            tensor.wait_ge(cx.sem_s, cx.s)
                            for ds in range(8):
                                cx.ti(tensor.matmul(
                                    P_h[:, 0:512],
                                    w1_sb[:, e * 4096 + ds * 512 + ft * 128:
                                          e * 4096 + ds * 512 + (ft + 1) * 128],
                                    xnT_blk[:, ds * TB0 + tc * 512: ds * TB0 + (tc + 1) * 512],
                                    start=(ds == 0), stop=(ds == 7)))

                        @block.scalar
                        def _(scalar, b=b, e=e, ft=ft, tc=tc):
                            scalar.wait_ge(cx.sem_t, cx.t)
                            cx.si(scalar.activation(
                                h_sb[:, ft * TB0 + tc * 512: ft * TB0 + (tc + 1) * 512],
                                P_h[:, 0:512], AF.Gelu,
                                bias=b1_sb[:, e * 4 + ft: e * 4 + ft + 1]))

                for tt in range(NT):
                    for dc in range(2):
                        @block.tensor
                        def _(tensor, b=b, e=e, tt=tt, dc=dc):
                            tensor.wait_ge(cx.sem_s, cx.s)
                            tensor.wait_ge(cx.sem_v, cx.v)
                            for fs in range(4):
                                cx.ti(tensor.matmul(
                                    P_y[:, 0:512],
                                    h_sb[:, fs * TB0 + tt * 128: fs * TB0 + (tt + 1) * 128],
                                    w2_sb[:, e * 4096 + fs * 1024 + dc * 512:
                                          e * 4096 + fs * 1024 + (dc + 1) * 512],
                                    start=(fs == 0), stop=(fs == 3)))

                        @block.vector
                        def _(vector, b=b, e=e, tt=tt, dc=dc):
                            vector.wait_ge(cx.sem_t, cx.t)
                            dst = acc[:, tt * D + dc * 512: tt * D + (dc + 1) * 512]
                            cx.vi(vector.scalar_tensor_tensor(
                                dst, P_y[:, 0:512], wsel[:, tt * NE + e: tt * NE + e + 1],
                                dst, ALU.mult, ALU.add))

            @block.gpsimd
            def _(gpsimd, b=b):
                gpsimd.wait_ge(cx.sem_v, cx.v)
                gpsimd.dma_start(
                    out=x1_d[b * TB0:(b + 1) * TB0, :].rearrange("(tt p) d -> p tt d", p=128),
                    in_=acc.rearrange("p (tt d) -> p tt d", tt=8),
                ).then_inc(cx.sem_st, 16)
                cx.st += 16
                if debug:
                    gpsimd.dma_start(out=wdump_d[b],
                                     in_=wsel[:]).then_inc(cx.sem_st, 16)
                    gpsimd.dma_start(out=gdump_d[b],
                                     in_=gdbg[:]).then_inc(cx.sem_st, 16)
                    cx.st += 32

        @block.gpsimd
        def _(gpsimd):
            gpsimd.wait_ge(cx.sem_st, cx.st)
    return nc


def build_b1(fused_nc=None, store=None, debug=False):
    nc = fused_nc or bass.Bass("TRN2", target_bir_lowering=False)
    ext = "ExternalInput"
    if debug:
        ksd_d = nc.dram_tensor("ksd", [T_FULL, D], BF16, kind="ExternalOutput").ap()
        vsd_d = nc.dram_tensor("vsd", [T_FULL, D], BF16, kind="ExternalOutput").ap()
    if store is None:
        x1_d = nc.dram_tensor("x1", [T_FULL, D], F32, kind=ext).ap()
    else:
        x1_d = store["x1"]
    wqkv_d = nc.dram_tensor("wqkv", [D, 3 * D], F32, kind=ext).ap()
    qnw_d = nc.dram_tensor("qnw_b", [128, D], F32, kind=ext).ap()
    knw_d = nc.dram_tensor("knw_b", [128, D], F32, kind=ext).ap()
    idb_d = nc.dram_tensor("idb3", [128, 128], BF16, kind=ext).ap()
    ones_d = nc.dram_tensor("ones_c", [128, 1], BF16, kind=ext).ap()
    okind = "Internal" if fused_nc else "ExternalOutput"
    xn1T_d = nc.dram_tensor("xn1T", [8, 128, T_FULL], BF16, kind=okind).ap()
    qs_d = nc.dram_tensor("qs", [T_FULL, D], BF16, kind=okind).ap()
    kv_d = nc.dram_tensor("kv", [128, 512], F32, kind=okind).ap()
    ksum_d = nc.dram_tensor("ksum", [D], F32, kind=okind).ap()
    if store is not None:
        store["xn1T"] = xn1T_d
        store["qs"] = qs_d
        store["kv"] = kv_d
        store["ksum"] = ksum_d

    cx = Ctx(nc) if fused_nc is None else store["cx"]
    _st0, _ld20 = cx.st, cx.ld2

    xin = [sb(nc, "b1_x0", [128, D], F32), sb(nc, "b1_x1", [128, D], F32)]
    tmpw = sb(nc, "b1_tmpw", [128, D], F32)
    xm = sb(nc, "b1_xm", [128, D], F32)
    xn_bf = sb(nc, "b1_xnbf", [128, D], BF16)
    xnT_sb = sb(nc, "b1_xnT", [128, D], BF16)
    msum = sb(nc, "b1_msum", [128, 1], F32)
    meanc = sb(nc, "b1_meanc", [128, 1], F32)
    vsum = sb(nc, "b1_vsum", [128, 1], F32)
    rstd = sb(nc, "b1_rstd", [128, 1], F32)
    junk = sb(nc, "b1_junk", [128, 1], F32)
    wqkv_sb = sb(nc, "b1_wqkv", [128, 8 * 3 * D], BF16)
    wld = sb(nc, "b1_wld", [128, 8192], F32)
    qnw_b = sb(nc, "b1_qnw", [128, D], F32)
    knw_b = sb(nc, "b1_knw", [128, D], F32)
    ident_b = sb(nc, "b1_idb", [128, 128], BF16)
    ones_c = sb(nc, "b1_ones", [128, 1], BF16)
    qs_bf = sb(nc, "b1_qsbf", [128, D], BF16)
    ks_bf = sb(nc, "b1_ksbf", [128, D], BF16)
    vs_bf = sb(nc, "b1_vsbf", [128, D], BF16)
    kv_acc = sb(nc, "b1_kvacc", [128, 512], F32)
    ksum_row = sb(nc, "b1_ksrow", [1, 1024], F32)
    qraw = sb(nc, "b1_qraw", [128, D], F32)
    junk2 = sb(nc, "b1_junk2", [128, 1], F32)

    P_q = ps(nc, "b1_Pq", [128, 1024], F32)
    P_kv = ps(nc, "b1_Pkv", [128, 512], F32)
    P_ks = ps(nc, "b1_Pks", [128, 1024], F32)
    P_bf3 = ps(nc, "b1_Pbf", [128, 1024], BF16)

    with nc.Block() as block:
        if fused_nc is not None:
            cx.barrier(block)
        @block.sync
        def _(sync):
            for (dst, src) in [(qnw_b, qnw_d), (knw_b, knw_d),
                               (ident_b, idb_d), (ones_c, ones_d)]:
                sync.dma_start(out=dst[:], in_=src).then_inc(cx.sem_ld, 16)
                cx.ld += 16

        for part in range(3):
            @block.sync
            def _(sync, part=part):
                sync.wait_ge(cx.sem_v, cx.v)
                sync.dma_start(
                    out=wld.rearrange("p (s n) -> p s n", s=8),
                    in_=wqkv_d[:, part * D:(part + 1) * D].rearrange(
                        "(s p) n -> p s n", p=128),
                ).then_inc(cx.sem_ld, 16)
                cx.ld += 16

            @block.vector
            def _(vector, part=part):
                vector.wait_ge(cx.sem_ld, cx.ld)
                for ds in range(8):
                    instr = vector.tensor_copy(
                        wqkv_sb[:, ds * 3 * D + part * D: ds * 3 * D + (part + 1) * D],
                        wld[:, ds * D:(ds + 1) * D])
                cx.vi(instr)

        @block.vector
        def _(vector):
            vector.memset(kv_acc[:], 0.0)
            cx.vi(vector.memset(ksum_row[:], 0.0))

        @block.sync
        def _(sync):
            sync.dma_start(out=xin[0][:], in_=x1_d[0:128, :]).then_inc(cx.sem_ld2, 16)
            cx.ld2 += 16

        for c in range(NCH):
            buf = c % 2
            x_sb = xin[buf]
            if c + 1 < NCH:
                @block.sync
                def _(sync, c=c, buf=buf):
                    if c >= 1:
                        sync.wait_ge(cx.sem_st, _st0 + 32 * c)
                    sync.dma_start(
                        out=xin[1 - buf][:],
                        in_=x1_d[(c + 1) * 128:(c + 2) * 128, :],
                    ).then_inc(cx.sem_ld2, 16)
                    cx.ld2 += 16

            @block.vector
            def _(vector, c=c, x_sb=x_sb):
                vector.wait_ge(cx.sem_ld2, _ld20 + 16 * (c + 1))
                if c > 0:
                    vector.wait_ge(cx.sem_t, cx.t)
                vector.reduce_sum(msum[:], x_sb[:], axis=AX.X)
                vector.tensor_copy(junk[:], msum[:])
                vector.tensor_scalar_mul(meanc[:], msum[:], 1.0 / D)
                vector.tensor_copy(junk[:], meanc[:])
                vector.tensor_scalar(xm[:], x_sb[:], meanc[:], None, ALU.subtract)
                vector.tensor_tensor(tmpw[:], xm[:], xm[:], ALU.mult)
                vector.reduce_sum(vsum[:], tmpw[:], axis=AX.X)
                vector.tensor_copy(junk[:], meanc[:])
                cx.vi(vector.tensor_scalar(rstd[:], vsum[:], 1.0 / D, EPS_LN,
                                           ALU.mult, ALU.add))

            @block.scalar
            def _(scalar):
                scalar.wait_ge(cx.sem_v, cx.v)
                cx.si(scalar.sqrt(rstd[:], rstd[:]))

            @block.vector
            def _(vector, c=c):
                vector.wait_ge(cx.sem_s, cx.s)
                vector.engine_nop()
                vector.reciprocal(rstd[:], rstd[:])
                vector.tensor_copy(junk[:], meanc[:])
                vector.tensor_scalar_mul(tmpw[:], xm[:], rstd[:])
                cx.vi(vector.tensor_copy(xn_bf[:], tmpw[:]))

            @block.tensor
            def _(tensor, c=c):
                tensor.wait_ge(cx.sem_v, cx.v)
                for ds in range(8):
                    cx.ti(tensor.transpose(
                        P_bf3[:, ds * 128:(ds + 1) * 128],
                        xn_bf[:, ds * 128:(ds + 1) * 128], ident_b[:]))

            @block.vector
            def _(vector, c=c):
                vector.wait_ge(cx.sem_t, cx.t)
                if c >= 2:
                    vector.wait_ge(cx.sem_st, _st0 + 32 * (c - 1) + 16)
                cx.vi(vector.tensor_copy(xnT_sb[:], P_bf3[:, 0:1024]))

            @block.gpsimd
            def _(gpsimd, c=c):
                gpsimd.wait_ge(cx.sem_v, cx.v)
                gpsimd.dma_start(
                    out=xn1T_d[:, :, c * 128:(c + 1) * 128].rearrange("s p t -> p s t"),
                    in_=xnT_sb.rearrange("p (s t) -> p s t", s=8),
                ).then_inc(cx.sem_st, 16)
                cx.st += 16

            for part, (dstb, gain) in enumerate(
                    [(qs_bf, qnw_b), (ks_bf, knw_b), (vs_bf, None)]):
                @block.tensor
                def _(tensor, c=c, part=part):
                    tensor.wait_ge(cx.sem_v, cx.v)
                    for half in range(2):
                        for ds in range(8):
                            cx.ti(tensor.matmul(
                                P_q[:, half * 512:(half + 1) * 512],
                                xnT_sb[:, ds * 128:(ds + 1) * 128],
                                wqkv_sb[:, ds * 3 * D + part * D + half * 512:
                                        ds * 3 * D + part * D + (half + 1) * 512],
                                start=(ds == 0), stop=(ds == 7)))

                @block.vector
                def _(vector, c=c, part=part, dstb=dstb, gain=gain):
                    vector.wait_ge(cx.sem_t, cx.t)
                    if gain is not None:
                        vector.tensor_copy(qraw[:], P_q[:, 0:1024])
                        vector.tensor_copy(junk[:], junk2[:])
                        vector.tensor_copy(junk2[:], junk[:])
                        vector.tensor_copy(junk[:], junk2[:])
                        vector.tensor_tensor(tmpw[:], qraw[:], qraw[:], ALU.mult)
                        vector.reduce_sum(vsum[:], tmpw[:], axis=AX.X)
                        vector.tensor_copy(junk[:], meanc[:])
                        cx.vi(vector.tensor_scalar(rstd[:], vsum[:], 1.0 / D, EPS_RMS,
                                                   ALU.mult, ALU.add))
                    else:
                        cx.vi(vector.tensor_copy(dstb[:], P_q[:, 0:1024]))

                if gain is not None:
                    @block.scalar
                    def _(scalar):
                        scalar.wait_ge(cx.sem_v, cx.v)
                        cx.si(scalar.sqrt(rstd[:], rstd[:]))

                    @block.vector
                    def _(vector, c=c, part=part, dstb=dstb, gain=gain):
                        vector.wait_ge(cx.sem_s, cx.s)
                        vector.engine_nop()
                        vector.reciprocal(rstd[:], rstd[:])
                        vector.tensor_copy(junk[:], meanc[:])
                        vector.tensor_scalar_mul(tmpw[:], qraw[:], rstd[:])
                        vector.tensor_tensor(tmpw[:], tmpw[:], gain[:], ALU.mult)
                        cx.vi(vector.tensor_scalar_max(dstb[:], tmpw[:], 0.0))

            @block.gpsimd
            def _(gpsimd, c=c):
                gpsimd.wait_ge(cx.sem_v, cx.v)
                gpsimd.dma_start(out=qs_d[c * 128:(c + 1) * 128, :],
                                 in_=qs_bf[:]).then_inc(cx.sem_st, 16)
                cx.st += 16
                if debug:
                    gpsimd.dma_start(out=ksd_d[c * 128:(c + 1) * 128, :],
                                     in_=ks_bf[:]).then_inc(cx.sem_ld, 16)
                    gpsimd.dma_start(out=vsd_d[c * 128:(c + 1) * 128, :],
                                     in_=vs_bf[:]).then_inc(cx.sem_ld, 16)
                    cx.ld += 32

            @block.tensor
            def _(tensor, c=c):
                tensor.wait_ge(cx.sem_v, cx.v)
                for h in [0, 2, 4, 6, 8, 10, 12, 14, 1, 3, 5, 7, 9, 11, 13, 15]:
                    po = (h % 2) * 64
                    fo = (h // 2) * 64
                    cx.ti(tensor.matmul(
                        P_kv[po:po + 64, fo:fo + 64],
                        ks_bf[:, h * 64:(h + 1) * 64],
                        vs_bf[:, h * 64:(h + 1) * 64],
                        start=True, stop=True))
                for half in range(2):
                    cx.ti(tensor.matmul(
                        P_ks[0:1, half * 512:(half + 1) * 512],
                        ones_c[:],
                        ks_bf[:, half * 512:(half + 1) * 512],
                        start=True, stop=True))

            @block.vector
            def _(vector, c=c):
                vector.wait_ge(cx.sem_t, cx.t)
                vector.tensor_tensor(kv_acc[:], kv_acc[:], P_kv[:, 0:512], ALU.add)
                cx.vi(vector.tensor_tensor(ksum_row[:], ksum_row[:],
                                           P_ks[0:1, 0:1024], ALU.add))

        @block.gpsimd
        def _(gpsimd):
            gpsimd.wait_ge(cx.sem_v, cx.v)
            gpsimd.dma_start(out=kv_d, in_=kv_acc[:]).then_inc(cx.sem_st, 16)
            gpsimd.dma_start(out=ksum_d.unsqueeze(0),
                             in_=ksum_row[:]).then_inc(cx.sem_st, 16)
            cx.st += 32
            gpsimd.wait_ge(cx.sem_st, cx.st)
    return nc


TB1 = 1024
NB1 = T_FULL // TB1
FS = 256
NFS = FF // FS


def build_b2(fused_nc=None, store=None):
    nc = fused_nc or bass.Bass("TRN2", target_bir_lowering=False)
    ext = "ExternalInput"
    if store is None:
        xn1T_d = nc.dram_tensor("xn1T", [8, 128, T_FULL], BF16, kind=ext).ap()
        x1_d = nc.dram_tensor("x1", [T_FULL, D], F32, kind=ext).ap()
    else:
        xn1T_d = store["xn1T"]
        x1_d = store["x1"]
    wg_d = nc.dram_tensor("wg", [D, NE], F32, kind=ext).ap()
    ws1_d = nc.dram_tensor("ws1", [D, FF], F32, kind=ext).ap()
    ws3_d = nc.dram_tensor("ws3", [D, FF], F32, kind=ext).ap()
    ws2_d = nc.dram_tensor("ws2", [FF, D], F32, kind=ext).ap()
    we1_d = nc.dram_tensor("we1", [NE, D, FF], F32, kind=ext).ap()
    we3_d = nc.dram_tensor("we3", [NE, D, FF], F32, kind=ext).ap()
    we2_d = nc.dram_tensor("we2", [NE, FF, D], F32, kind=ext).ap()
    okind = "Internal" if fused_nc else "ExternalOutput"
    x2p_d = nc.dram_tensor("x2p", [T_FULL, D], F32, kind=okind).ap()
    if store is not None:
        store["x2p"] = x2p_d

    cx = Ctx(nc) if fused_nc is None else store["cx"]
    _st0, _ld20 = cx.st, cx.ld2
    NT = TB1 // 128

    xnT_blk = sb(nc, "b2_xnT", [128, 8 * TB1], BF16)
    acc = sb(nc, "b2_acc", [128, NT * D], F32)
    h_sb = sb(nc, "b2_h", [128, 16 * TB1], BF16)
    gw_sb = sb(nc, "b2_gw", [128, 8 * NE], BF16)
    wld_g = sb(nc, "b2_wldg", [128, 64], F32)
    w13ld = [sb(nc, "b2_w13a", [128, 2 * 8 * FS], F32),
             sb(nc, "b2_w13b", [128, 2 * 8 * FS], F32)]
    w2ld = [sb(nc, "b2_w2a", [128, 4096], F32),
            sb(nc, "b2_w2b", [128, 4096], F32)]
    w1_bf = sb(nc, "b2_w1bf", [128, 8 * FS], BF16)
    w3_bf = sb(nc, "b2_w3bf", [128, 8 * FS], BF16)
    w2_bf = sb(nc, "b2_w2bf", [128, 4096], BF16)
    sl_bf = sb(nc, "b2_slbf", [128, 512], BF16)
    gl_all = sb(nc, "b2_glall", [128, NT * NE], F32)
    g_all = sb(nc, "b2_gall", [128, NT * NE], F32)
    msk_all = sb(nc, "b2_mskall", [128, NT * NE], F32)
    mskd_all = sb(nc, "b2_mskdall", [128, NT * NE], F32)
    selt_all = sb(nc, "b2_seltall", [128, NT * NE], F32)
    sg_all = sb(nc, "b2_sgall", [128, NT * NE], F32)
    m1L = sb(nc, "b2_m1L", [128, NT], F32)
    m2L = sb(nc, "b2_m2L", [128, NT], F32)
    den8 = sb(nc, "b2_den8", [128, NT], F32)
    rd8 = sb(nc, "b2_rd8", [128, NT], F32)
    wsel = sb(nc, "b2_wsel", [128, NT * NE], F32)
    junk = sb(nc, "b2_junk", [128, 1], F32)
    junk2 = sb(nc, "b2_junk2", [128, 1], F32)

    P_h1 = ps(nc, "b2_Ph1", [128, 512], F32)
    P_h3 = ps(nc, "b2_Ph3", [128, 512], F32)
    P_y = ps(nc, "b2_Py", [128, 512], F32)
    P_g = ps(nc, "b2_Pg", [128, 512], F32)

    def w13_src(e, f0, f1):
        if e < 0:
            return ws1_d[:, f0:f1], ws3_d[:, f0:f1]
        return we1_d[e][:, f0:f1], we3_d[e][:, f0:f1]

    def w2_src(e, f0, f1):
        if e < 0:
            return ws2_d[f0:f1, :]
        return we2_d[e][f0:f1, :]

    with nc.Block() as block:
        if fused_nc is not None:
            cx.barrier(block)
        @block.sync
        def _(sync):
            sync.dma_start(
                out=wld_g.rearrange("p (s g) -> p s g", s=8),
                in_=wg_d.rearrange("(s p) g -> p s g", p=128),
            ).then_inc(cx.sem_ld, 16)
            cx.ld += 16

        @block.vector
        def _(vector):
            vector.wait_ge(cx.sem_ld, cx.ld)
            cx.vi(vector.tensor_copy(gw_sb[:], wld_g[:]))

        experts = [-1] + list(range(NE))
        for b in range(NB1):
            @block.sync
            def _(sync, b=b):
                if b > 0:
                    sync.wait_ge(cx.sem_st, cx.st)
                sync.dma_start(
                    out=xnT_blk.rearrange("p (s t) -> p s t", s=8),
                    in_=xn1T_d[:, :, b * TB1:(b + 1) * TB1].rearrange("s p t -> p s t"),
                ).then_inc(cx.sem_ld2, 16)
                sync.dma_start(
                    out=acc.rearrange("p (tt d) -> p tt d", tt=NT),
                    in_=x1_d[b * TB1:(b + 1) * TB1, :].rearrange(
                        "(tt p) d -> p tt d", p=128),
                ).then_inc(cx.sem_ld2, 16)
                cx.ld2 += 32

            @block.tensor
            def _(tensor, b=b):
                tensor.wait_ge(cx.sem_ld2, cx.ld2)
                tensor.wait_ge(cx.sem_v, cx.v)
                for tt in range(NT):
                    for ds in range(8):
                        cx.ti(tensor.matmul(
                            P_g[:, tt * NE:(tt + 1) * NE],
                            xnT_blk[:, ds * TB1 + tt * 128: ds * TB1 + (tt + 1) * 128],
                            gw_sb[:, ds * NE:(ds + 1) * NE],
                            start=(ds == 0), stop=(ds == 7)))

            @block.vector
            def _(vector, b=b):
                vector.wait_ge(cx.sem_t, cx.t)
                vector.tensor_copy(gl_all[:], P_g[:, 0:NT * NE])
                vector.tensor_copy(junk[:], junk2[:])
                vector.tensor_copy(junk2[:], junk[:])
                cx.vi(vector.tensor_copy(junk[:], junk2[:]))

            @block.scalar
            def _(scalar, b=b):
                scalar.wait_ge(cx.sem_v, cx.v)
                cx.si(scalar.activation(g_all[:], gl_all[:], AF.Sigmoid))

            @block.vector
            def _(vector, b=b):
                vector.wait_ge(cx.sem_s, cx.s)
                for tt in range(NT):
                    vector.reduce_max(m1L[:, tt:tt + 1],
                                      gl_all[:, tt * NE:(tt + 1) * NE], axis=AX.X)
                for tt in range(NT):
                    vector.tensor_scalar(msk_all[:, tt * NE:(tt + 1) * NE],
                                         gl_all[:, tt * NE:(tt + 1) * NE],
                                         m1L[:, tt:tt + 1], None, ALU.is_equal)
                for tt in range(NT):
                    vector.scalar_tensor_tensor(
                        mskd_all[:, tt * NE:(tt + 1) * NE],
                        msk_all[:, tt * NE:(tt + 1) * NE], -1e30,
                        gl_all[:, tt * NE:(tt + 1) * NE], ALU.mult, ALU.add)
                for tt in range(NT):
                    vector.reduce_max(m2L[:, tt:tt + 1],
                                      mskd_all[:, tt * NE:(tt + 1) * NE], axis=AX.X)
                for tt in range(NT):
                    vector.tensor_scalar(selt_all[:, tt * NE:(tt + 1) * NE],
                                         gl_all[:, tt * NE:(tt + 1) * NE],
                                         m2L[:, tt:tt + 1], None, ALU.is_ge)
                vector.tensor_copy(junk[:], junk2[:])
                vector.tensor_copy(junk2[:], junk[:])
                vector.tensor_copy(junk[:], junk2[:])
                vector.tensor_tensor(sg_all[:], selt_all[:], g_all[:], ALU.mult)
                vector.tensor_copy(junk[:], junk2[:])
                vector.tensor_copy(junk2[:], junk[:])
                vector.tensor_copy(junk[:], junk2[:])
                for tt in range(NT):
                    vector.reduce_sum(den8[:, tt:tt + 1],
                                      sg_all[:, tt * NE:(tt + 1) * NE], axis=AX.X)
                vector.tensor_copy(junk[:], junk2[:])
                vector.tensor_copy(junk2[:], junk[:])
                vector.tensor_copy(junk[:], junk2[:])
                vector.reciprocal(rd8[:], den8[:])
                vector.tensor_copy(junk[:], junk2[:])
                vector.tensor_copy(junk2[:], junk[:])
                vector.tensor_copy(junk[:], junk2[:])
                for tt in range(NT):
                    instr = vector.tensor_scalar(
                        wsel[:, tt * NE:(tt + 1) * NE],
                        sg_all[:, tt * NE:(tt + 1) * NE], rd8[:, tt:tt + 1],
                        1.0 / 3.0, ALU.mult, ALU.mult)
                cx.vi(instr)

            for ei, e in enumerate(experts):
                for q in range(NFS):
                    lbuf = q % 2
                    @block.sync
                    def _(sync, b=b, e=e, q=q, lbuf=lbuf):
                        sync.wait_ge(cx.sem_v, cx.v)
                        s1, s3 = w13_src(e, q * FS, (q + 1) * FS)
                        sync.dma_start(
                            out=w13ld[lbuf][:, 0:8 * FS].rearrange(
                                "p (s f) -> p s f", s=8),
                            in_=s1.rearrange("(s p) f -> p s f", p=128),
                        ).then_inc(cx.sem_ld2, 16)
                        sync.dma_start(
                            out=w13ld[lbuf][:, 8 * FS:16 * FS].rearrange(
                                "p (s f) -> p s f", s=8),
                            in_=s3.rearrange("(s p) f -> p s f", p=128),
                        ).then_inc(cx.sem_ld2, 16)
                        cx.ld2 += 32

                    @block.vector
                    def _(vector, b=b, e=e, q=q, lbuf=lbuf):
                        vector.wait_ge(cx.sem_ld2, cx.ld2)
                        vector.wait_ge(cx.sem_t, cx.t)
                        vector.tensor_copy(w1_bf[:], w13ld[lbuf][:, 0:8 * FS])
                        cx.vi(vector.tensor_copy(w3_bf[:], w13ld[lbuf][:, 8 * FS:16 * FS]))

                    for ft in range(FS // 128):
                        for tc in range(TB1 // 512):
                            @block.tensor
                            def _(tensor, b=b, e=e, q=q, ft=ft, tc=tc):
                                tensor.wait_ge(cx.sem_v, cx.v)
                                for ds in range(8):
                                    cx.ti(tensor.matmul(
                                        P_h1[:, 0:512],
                                        w1_bf[:, ds * FS + ft * 128: ds * FS + (ft + 1) * 128],
                                        xnT_blk[:, ds * TB1 + tc * 512: ds * TB1 + (tc + 1) * 512],
                                        start=(ds == 0), stop=(ds == 7)))
                                for ds in range(8):
                                    cx.ti(tensor.matmul(
                                        P_h3[:, 0:512],
                                        w3_bf[:, ds * FS + ft * 128: ds * FS + (ft + 1) * 128],
                                        xnT_blk[:, ds * TB1 + tc * 512: ds * TB1 + (tc + 1) * 512],
                                        start=(ds == 0), stop=(ds == 7)))

                            @block.scalar
                            def _(scalar, b=b, e=e, q=q, ft=ft, tc=tc):
                                scalar.wait_ge(cx.sem_t, cx.t)
                                cx.si(scalar.activation(sl_bf[:], P_h1[:, 0:512], AF.Silu))

                            @block.vector
                            def _(vector, b=b, e=e, q=q, ft=ft, tc=tc):
                                vector.wait_ge(cx.sem_s, cx.s)
                                fslot = q * FS + ft * 128
                                cx.vi(vector.tensor_tensor(
                                    h_sb[:, (fslot // 128) * TB1 + tc * 512:
                                         (fslot // 128) * TB1 + (tc + 1) * 512],
                                    sl_bf[:], P_h3[:, 0:512], ALU.mult))

                for fh in range(2):
                    for dc in range(2):
                        lbuf = (fh * 2 + dc) % 2
                        @block.sync
                        def _(sync, b=b, e=e, fh=fh, dc=dc, lbuf=lbuf):
                            sync.wait_ge(cx.sem_v, cx.v)
                            s2 = w2_src(e, fh * 1024, (fh + 1) * 1024)[:, dc * 512:(dc + 1) * 512]
                            sync.dma_start(
                                out=w2ld[lbuf].rearrange("p (s dd) -> p s dd", s=8),
                                in_=s2.rearrange("(s p) dd -> p s dd", p=128),
                            ).then_inc(cx.sem_ld2, 16)
                            cx.ld2 += 16

                        @block.vector
                        def _(vector, b=b, e=e, fh=fh, dc=dc, lbuf=lbuf):
                            vector.wait_ge(cx.sem_ld2, cx.ld2)
                            vector.wait_ge(cx.sem_t, cx.t)
                            cx.vi(vector.tensor_copy(w2_bf[:], w2ld[lbuf][:]))

                        for tt in range(NT):
                            @block.tensor
                            def _(tensor, b=b, e=e, fh=fh, dc=dc, tt=tt):
                                tensor.wait_ge(cx.sem_v, cx.v)
                                for fs in range(8):
                                    fglob = fh * 8 + fs
                                    cx.ti(tensor.matmul(
                                        P_y[:, 0:512],
                                        h_sb[:, fglob * TB1 + tt * 128:
                                             fglob * TB1 + (tt + 1) * 128],
                                        w2_bf[:, fs * 512:(fs + 1) * 512],
                                        start=(fs == 0), stop=(fs == 7)))

                            @block.vector
                            def _(vector, b=b, e=e, fh=fh, dc=dc, tt=tt):
                                vector.wait_ge(cx.sem_t, cx.t)
                                dst = acc[:, tt * D + dc * 512: tt * D + (dc + 1) * 512]
                                if e < 0:
                                    cx.vi(vector.scalar_tensor_tensor(
                                        dst, P_y[:, 0:512], 1.0 / 3.0, dst,
                                        ALU.mult, ALU.add))
                                else:
                                    cx.vi(vector.scalar_tensor_tensor(
                                        dst, P_y[:, 0:512],
                                        wsel[:, tt * NE + e: tt * NE + e + 1],
                                        dst, ALU.mult, ALU.add))

            @block.gpsimd
            def _(gpsimd, b=b):
                gpsimd.wait_ge(cx.sem_v, cx.v)
                gpsimd.dma_start(
                    out=x2p_d[b * TB1:(b + 1) * TB1, :].rearrange(
                        "(tt p) d -> p tt d", p=128),
                    in_=acc.rearrange("p (tt d) -> p tt d", tt=NT),
                ).then_inc(cx.sem_st, 16)
                cx.st += 16

        @block.gpsimd
        def _(gpsimd):
            gpsimd.wait_ge(cx.sem_st, cx.st)
    return nc


B3_CUT = 99


def build_b3(fused_nc=None, store=None):
    nc = fused_nc or bass.Bass("TRN2", target_bir_lowering=False)
    ext = "ExternalInput"
    if store is None:
        qs_d = nc.dram_tensor("qs", [T_FULL, D], BF16, kind=ext).ap()
        kv_d = nc.dram_tensor("kv", [128, 512], F32, kind=ext).ap()
        ksum_d = nc.dram_tensor("ksum", [D], F32, kind=ext).ap()
        x2p_d = nc.dram_tensor("x2p", [T_FULL, D], F32, kind=ext).ap()
    else:
        qs_d = store["qs"]
        kv_d = store["kv"]
        ksum_d = store["ksum"]
        x2p_d = store["x2p"]
    wout_d = nc.dram_tensor("wout", [D, D], F32, kind=ext).ap()
    bout_d = nc.dram_tensor("bout_b", [128, D], F32, kind=ext).ap()
    idb_d = nc.dram_tensor("idb4", [128, 128], BF16, kind=ext).ap()
    onesr_d = nc.dram_tensor("onesr", [1, 128], BF16, kind=ext).ap()
    x2_d = nc.dram_tensor("x2", [T_FULL, D], F32, kind="ExternalOutput").ap()
    if store is not None:
        store["x2"] = x2_d

    cx = Ctx(nc) if fused_nc is None else store["cx"]
    _st0, _ld20 = cx.st, cx.ld2

    qs_sb = [sb(nc, "b3_qs0", [128, D], BF16), sb(nc, "b3_qs1", [128, D], BF16)]
    x2p_sb = [sb(nc, "b3_xp0", [128, D], F32), sb(nc, "b3_xp1", [128, D], F32)]
    kvbd_sb = sb(nc, "b3_kvbd", [128, 1024], BF16)
    kv_ld = sb(nc, "b3_kvld", [128, 512], F32)
    ksum_b = sb(nc, "b3_ksb", [128, D], F32)
    krow = sb(nc, "b3_krow", [1, D], F32)
    krow_bf = sb(nc, "b3_krowbf", [1, D], BF16)
    ones_r = sb(nc, "b3_onesr", [1, 128], BF16)
    wout_sb = sb(nc, "b3_wout", [128, 8 * D], BF16)
    wld = sb(nc, "b3_wld", [128, 8 * D], F32)
    bout_b = sb(nc, "b3_bout", [128, D], F32)
    ident_b = sb(nc, "b3_idb", [128, 128], BF16)
    tmpw = sb(nc, "b3_tmpw", [128, D], F32)
    nrm = sb(nc, "b3_nrm", [128, HEADS], F32)
    rn = sb(nc, "b3_rn", [128, HEADS], F32)
    junk = sb(nc, "b3_junk", [128, 1], F32)
    junk2 = sb(nc, "b3_junk2", [128, 1], F32)
    qsn_bf = sb(nc, "b3_qsn", [128, D], BF16)
    qsnT = sb(nc, "b3_qsnT", [128, D], BF16)
    o_bf = sb(nc, "b3_obf", [128, D], BF16)
    oT = sb(nc, "b3_oT", [128, D], BF16)
    x2_sb = [sb(nc, "b3_x20", [128, D], F32), sb(nc, "b3_x21", [128, D], F32)]

    P_o = ps(nc, "b3_Po", [128, 1024], F32)
    P_w = ps(nc, "b3_Pw", [128, 1024], F32)
    P_bf4 = ps(nc, "b3_Pbf", [128, 1024], BF16)

    with nc.Block() as block:
        if fused_nc is not None:
            cx.barrier(block)
        @block.sync
        def _(sync):
            sync.dma_start(out=ident_b[:], in_=idb_d).then_inc(cx.sem_ld, 16)
            sync.dma_start(out=bout_b[:], in_=bout_d).then_inc(cx.sem_ld, 16)
            sync.dma_start(out=kv_ld[:], in_=kv_d).then_inc(cx.sem_ld, 16)
            sync.dma_start(out=krow[:], in_=ksum_d.unsqueeze(0)
                           ).then_inc(cx.sem_ld, 16)
            sync.dma_start(out=ones_r[:], in_=onesr_d).then_inc(cx.sem_ld, 16)
            sync.dma_start(out=wld.rearrange("p (s n) -> p s n", s=8),
                           in_=wout_d.rearrange("(s p) n -> p s n", p=128)
                           ).then_inc(cx.sem_ld, 16)
            cx.ld += 96

        @block.vector
        def _(vector):
            vector.wait_ge(cx.sem_ld, cx.ld)
            vector.memset(kvbd_sb[:], 0.0)
            for dt in range(8):
                vector.tensor_copy(kvbd_sb[0:64, dt * 128:dt * 128 + 64],
                                   kv_ld[0:64, dt * 64:(dt + 1) * 64])
                vector.tensor_copy(kvbd_sb[64:128, dt * 128 + 64:dt * 128 + 128],
                                   kv_ld[64:128, dt * 64:(dt + 1) * 64])
            vector.tensor_copy(krow_bf[:], krow[:])
            cx.vi(vector.tensor_copy(wout_sb[:], wld[:]))

        @block.tensor
        def _(tensor):
            tensor.wait_ge(cx.sem_v, cx.v)
            for half in range(2):
                cx.ti(tensor.matmul(P_w[:, half * 512:(half + 1) * 512], ones_r[:],
                                    krow_bf[:, half * 512:(half + 1) * 512],
                                    start=True, stop=True))

        @block.vector
        def _(vector):
            vector.wait_ge(cx.sem_t, cx.t)
            cx.vi(vector.tensor_copy(ksum_b[:], P_w[:, 0:1024]))

        @block.sync
        def _(sync):
            sync.dma_start(out=qs_sb[0][:], in_=qs_d[0:128, :]).then_inc(cx.sem_ld2, 16)
            sync.dma_start(out=x2p_sb[0][:], in_=x2p_d[0:128, :]).then_inc(cx.sem_ld2, 16)
            cx.ld2 += 32

        for c in range(NCH):
            buf = c % 2
            if c + 1 < NCH:
                @block.sync
                def _(sync, c=c, buf=buf):
                    if c >= 1:
                        sync.wait_ge(cx.sem_st, _st0 + 16 * c)
                    sync.dma_start(out=qs_sb[1 - buf][:],
                                   in_=qs_d[(c + 1) * 128:(c + 2) * 128, :]
                                   ).then_inc(cx.sem_ld2, 16)
                    sync.dma_start(out=x2p_sb[1 - buf][:],
                                   in_=x2p_d[(c + 1) * 128:(c + 2) * 128, :]
                                   ).then_inc(cx.sem_ld2, 16)
                    cx.ld2 += 32

            @block.vector
            def _(vector, c=c, buf=buf):
                vector.wait_ge(cx.sem_ld2, _ld20 + 32 * (c + 1))
                if c > 0:
                    vector.wait_ge(cx.sem_t, cx.t)
                vector.tensor_tensor(tmpw[:], qs_sb[buf][:], ksum_b[:], ALU.mult)
                vector.tensor_copy(junk[:], junk2[:])
                vector.tensor_copy(junk2[:], junk[:])
                vector.tensor_copy(junk[:], junk2[:])
                for h in range(HEADS):
                    vector.reduce_sum(nrm[:, h:h + 1],
                                      tmpw[:, h * DH:(h + 1) * DH], axis=AX.X)
                vector.tensor_copy(junk[:], junk2[:])
                vector.tensor_copy(junk2[:], junk[:])
                vector.tensor_copy(junk[:], junk2[:])
                vector.tensor_scalar_add(nrm[:], nrm[:], EPS_DEN)
                vector.tensor_copy(junk[:], junk2[:])
                vector.tensor_copy(junk2[:], junk[:])
                vector.tensor_copy(junk[:], junk2[:])
                vector.reciprocal(rn[:], nrm[:])
                vector.tensor_copy(junk[:], junk2[:])
                vector.tensor_copy(junk2[:], junk[:])
                vector.tensor_copy(junk[:], junk2[:])
                for h in range(HEADS):
                    vector.tensor_scalar_mul(
                        qsn_bf[:, h * DH:(h + 1) * DH],
                        qs_sb[buf][:, h * DH:(h + 1) * DH], rn[:, h:h + 1])
                vector.tensor_copy(junk[:], junk2[:])
                cx.vi(vector.tensor_copy(junk2[:], junk[:]))

            if B3_CUT <= 1:
                @block.gpsimd
                def _(gpsimd, c=c, buf=buf):
                    gpsimd.wait_ge(cx.sem_v, cx.v)
                    gpsimd.dma_start(out=x2_d[c * 128:(c + 1) * 128, :],
                                     in_=x2p_sb[buf][:]).then_inc(cx.sem_st, 16)
                    cx.st += 16
                continue

            @block.tensor
            def _(tensor, c=c):
                tensor.wait_ge(cx.sem_v, cx.v)
                for ds in range(8):
                    cx.ti(tensor.transpose(
                        P_bf4[:, ds * 128:(ds + 1) * 128],
                        qsn_bf[:, ds * 128:(ds + 1) * 128], ident_b[:]))

            @block.vector
            def _(vector, c=c):
                vector.wait_ge(cx.sem_t, cx.t)
                cx.vi(vector.tensor_copy(qsnT[:], P_bf4[:, 0:1024]))

            if B3_CUT <= 2:
                @block.gpsimd
                def _(gpsimd, c=c, buf=buf):
                    gpsimd.wait_ge(cx.sem_v, cx.v)
                    gpsimd.dma_start(out=x2_d[c * 128:(c + 1) * 128, :],
                                     in_=x2p_sb[buf][:]).then_inc(cx.sem_st, 16)
                    cx.st += 16
                continue

            @block.tensor
            def _(tensor, c=c):
                tensor.wait_ge(cx.sem_v, cx.v)
                for dt in range(8):
                    cx.ti(tensor.matmul(
                        P_o[:, dt * 128:(dt + 1) * 128],
                        qsnT[:, dt * 128:(dt + 1) * 128],
                        kvbd_sb[:, dt * 128:(dt + 1) * 128],
                        start=True, stop=True))

            @block.vector
            def _(vector, c=c):
                vector.wait_ge(cx.sem_t, cx.t)
                cx.vi(vector.tensor_copy(o_bf[:], P_o[:, 0:1024]))

            if B3_CUT <= 3:
                @block.gpsimd
                def _(gpsimd, c=c, buf=buf):
                    gpsimd.wait_ge(cx.sem_v, cx.v)
                    gpsimd.dma_start(out=x2_d[c * 128:(c + 1) * 128, :],
                                     in_=x2p_sb[buf][:]).then_inc(cx.sem_st, 16)
                    cx.st += 16
                continue

            @block.tensor
            def _(tensor, c=c):
                tensor.wait_ge(cx.sem_v, cx.v)
                for ds in range(8):
                    cx.ti(tensor.transpose(
                        P_bf4[:, ds * 128:(ds + 1) * 128],
                        o_bf[:, ds * 128:(ds + 1) * 128], ident_b[:]))

            @block.vector
            def _(vector, c=c):
                vector.wait_ge(cx.sem_t, cx.t)
                cx.vi(vector.tensor_copy(oT[:], P_bf4[:, 0:1024]))

            @block.tensor
            def _(tensor, c=c):
                tensor.wait_ge(cx.sem_v, cx.v)
                for half in range(2):
                    for ds in range(8):
                        cx.ti(tensor.matmul(
                            P_w[:, half * 512:(half + 1) * 512],
                            oT[:, ds * 128:(ds + 1) * 128],
                            wout_sb[:, ds * D + half * 512: ds * D + (half + 1) * 512],
                            start=(ds == 0), stop=(ds == 7)))

            @block.vector
            def _(vector, c=c, buf=buf):
                vector.wait_ge(cx.sem_t, cx.t)
                if c >= 2:
                    vector.wait_ge(cx.sem_st, _st0 + 16 * (c - 1))
                vector.tensor_tensor(tmpw[:], P_w[:, 0:1024], bout_b[:], ALU.add)
                cx.vi(vector.tensor_tensor(x2_sb[buf][:], tmpw[:], x2p_sb[buf][:],
                                           ALU.add))

            @block.gpsimd
            def _(gpsimd, c=c, buf=buf):
                gpsimd.wait_ge(cx.sem_v, cx.v)
                gpsimd.dma_start(out=x2_d[c * 128:(c + 1) * 128, :],
                                 in_=x2_sb[buf][:]).then_inc(cx.sem_st, 16)
                cx.st += 16

        @block.gpsimd
        def _(gpsimd):
            gpsimd.wait_ge(cx.sem_st, cx.st)
    return nc


def _consts():
    return {
        "idf": np.eye(128, dtype=np.float32),
        "idb": np.eye(128, dtype=np.float32).astype(ml_dtypes.bfloat16),
        "msl": np.tril(np.ones((128, 128), np.float32), -1),
        "msu": np.triu(np.ones((128, 128), np.float32), 1),
        "mui": np.triu(np.ones((128, 128), np.float32), 0),
    }


def _a1_maps(inputs):
    cn = _consts()
    maps = []
    for b in range(8):
        maps.append(dict(
            x=np.ascontiguousarray(inputs["L0_attn_x"][b]),
            wqT=np.ascontiguousarray(inputs["L0_attn_Wq"].T),
            wkT=np.ascontiguousarray(inputs["L0_attn_Wk"].T),
            wvT=np.ascontiguousarray(inputs["L0_attn_Wv"].T),
            woT=np.ascontiguousarray(inputs["L0_attn_Wo"].T),
            bq_b=np.ascontiguousarray(np.broadcast_to(inputs["L0_attn_bq"], (128, H))),
            bk_b=np.ascontiguousarray(np.broadcast_to(inputs["L0_attn_bk"], (128, H))),
            bv_b=np.ascontiguousarray(np.broadcast_to(inputs["L0_attn_bv"], (128, H))),
            bo_b=np.ascontiguousarray(np.broadcast_to(inputs["L0_attn_bo"], (128, D))),
            **cn,
        ))
    return maps


def _a2_maps(inputs, prev):
    b1l = np.ascontiguousarray(
        inputs["L0_ffn_b1"].reshape(NE, 4, 128).transpose(2, 0, 1).reshape(128, NE * 4))
    maps = []
    for b in range(8):
        maps.append(dict(
            xnT=prev[b]["xnT"],
            x1p=prev[b]["x1p"],
            gw=inputs["L0_ffn_gate_w"],
            w1=inputs["L0_ffn_w1"],
            b1l=b1l,
            w2=inputs["L0_ffn_w2"],
            b2l=inputs["L0_ffn_b2"],
            idb2=_consts()["idb"],
        ))
    return maps


def _b1_maps(inputs, prev):
    cn = _consts()
    maps = []
    for b in range(8):
        maps.append(dict(
            x1=prev[b]["x1"],
            wqkv=inputs["L1_attn_W_qkv"],
            qnw_b=np.ascontiguousarray(np.broadcast_to(inputs["L1_attn_qn_w"], (128, D))),
            knw_b=np.ascontiguousarray(np.broadcast_to(inputs["L1_attn_kn_w"], (128, D))),
            idb3=cn["idb"],
            ones_c=np.ones((128, 1), dtype=ml_dtypes.bfloat16),
        ))
    return maps


def _b2_maps(inputs, prev):
    maps = []
    for b in range(8):
        maps.append(dict(
            xn1T=prev[b]["xn1T"],
            x1=prev[b]["x1"],
            wg=inputs["L1_ffn_Wg"],
            ws1=inputs["L1_ffn_Ws1"],
            ws3=inputs["L1_ffn_Ws3"],
            ws2=inputs["L1_ffn_Ws2"],
            we1=inputs["L1_ffn_We1"],
            we3=inputs["L1_ffn_We3"],
            we2=inputs["L1_ffn_We2"],
        ))
    return maps


def _b3_maps(inputs, prev):
    cn = _consts()
    maps = []
    for b in range(8):
        maps.append(dict(
            qs=prev[b]["qs"],
            kv=prev[b]["kv"],
            ksum=prev[b]["ksum"],
            x2p=prev[b]["x2p"],
            wout=inputs["L1_attn_W_out"],
            bout_b=np.ascontiguousarray(np.broadcast_to(inputs["L1_attn_b_out"], (128, D))),
            idb4=cn["idb"],
            onesr=np.ones((1, 128), dtype=ml_dtypes.bfloat16),
        ))
    return maps


def run_a1(inputs):
    nc = build_a1()
    res = run_bass_kernel_spmd(nc, _a1_maps(inputs), core_ids=list(range(8)))
    return res.results


def kernel_unfused(**inputs):
    r1 = run_a1(inputs)
    nc2 = build_a2()
    r2 = run_bass_kernel_spmd(nc2, _a2_maps(inputs, r1), core_ids=list(range(8))).results
    nc3 = build_b1()
    r3 = run_bass_kernel_spmd(nc3, _b1_maps(inputs, r2), core_ids=list(range(8))).results
    merged = [dict(r2[b], **r3[b]) for b in range(8)]
    nc4 = build_b2()
    r4 = run_bass_kernel_spmd(nc4, _b2_maps(inputs, merged), core_ids=list(range(8))).results
    merged2 = [dict(merged[b], **r4[b]) for b in range(8)]
    nc5 = build_b3()
    r5 = run_bass_kernel_spmd(nc5, _b3_maps(inputs, merged2), core_ids=list(range(8))).results
    out = np.stack([r5[b]["x2"] for b in range(8)], axis=0).astype(np.float32)
    return out


def _fused_maps(inputs):
    cn = _consts()
    b1l = np.ascontiguousarray(
        inputs["L0_ffn_b1"].reshape(NE, 4, 128).transpose(2, 0, 1).reshape(128, NE * 4))
    maps = []
    for b in range(8):
        maps.append(dict(
            x=np.ascontiguousarray(inputs["L0_attn_x"][b]),
            wqT=np.ascontiguousarray(inputs["L0_attn_Wq"].T),
            wkT=np.ascontiguousarray(inputs["L0_attn_Wk"].T),
            wvT=np.ascontiguousarray(inputs["L0_attn_Wv"].T),
            woT=np.ascontiguousarray(inputs["L0_attn_Wo"].T),
            bq_b=np.ascontiguousarray(np.broadcast_to(inputs["L0_attn_bq"], (128, H))),
            bk_b=np.ascontiguousarray(np.broadcast_to(inputs["L0_attn_bk"], (128, H))),
            bv_b=np.ascontiguousarray(np.broadcast_to(inputs["L0_attn_bv"], (128, H))),
            bo_b=np.ascontiguousarray(np.broadcast_to(inputs["L0_attn_bo"], (128, D))),
            gw=inputs["L0_ffn_gate_w"],
            w1=inputs["L0_ffn_w1"],
            b1l=b1l,
            w2=inputs["L0_ffn_w2"],
            b2l=inputs["L0_ffn_b2"],
            idb2=cn["idb"],
            wqkv=inputs["L1_attn_W_qkv"],
            qnw_b=np.ascontiguousarray(np.broadcast_to(inputs["L1_attn_qn_w"], (128, D))),
            knw_b=np.ascontiguousarray(np.broadcast_to(inputs["L1_attn_kn_w"], (128, D))),
            idb3=cn["idb"],
            ones_c=np.ones((128, 1), dtype=ml_dtypes.bfloat16),
            wg=inputs["L1_ffn_Wg"],
            ws1=inputs["L1_ffn_Ws1"],
            ws3=inputs["L1_ffn_Ws3"],
            ws2=inputs["L1_ffn_Ws2"],
            we1=inputs["L1_ffn_We1"],
            we3=inputs["L1_ffn_We3"],
            we2=inputs["L1_ffn_We2"],
            wout=inputs["L1_attn_W_out"],
            bout_b=np.ascontiguousarray(np.broadcast_to(inputs["L1_attn_b_out"], (128, D))),
            idb4=cn["idb"],
            onesr=np.ones((1, 128), dtype=ml_dtypes.bfloat16),
            **cn,
        ))
    return maps


def build_fused():
    global _ARENA
    nc = bass.Bass("TRN2", target_bir_lowering=False)
    store = {}
    for build in (build_a1, build_a2, build_b1, build_b2, build_b3):
        _ARENA = contextlib.ExitStack()
        build(fused_nc=nc, store=store)
        _ARENA.close()
    _ARENA = None
    return nc


def kernel(**inputs):
    nc = build_fused()
    res = run_bass_kernel_spmd(nc, _fused_maps(inputs), core_ids=list(range(8)))
    out = np.stack([res.results[b]["x2"] for b in range(8)], axis=0).astype(np.float32)
    return out


# version 08
# speedup vs baseline: 1.3151x; 1.3151x over previous
import contextlib
import numpy as np
import ml_dtypes
import concourse.bass as bass
import concourse.mybir as mybir
from concourse.bass_utils import run_bass_kernel_spmd

F32 = mybir.dt.float32
BF16 = mybir.dt.bfloat16
AF = mybir.ActivationFunctionType
ALU = mybir.AluOpType
AX = mybir.AxisListType

T_FULL = 8192
D = 1024
H = 128
C = 128
NCH = T_FULL // C
EPS_SCAN = 1e-6
EPS_LN = 1e-5
EPS_RMS = 1e-6
EPS_DEN = 1e-6
NE = 8
ECAP = 512
FF = 2048
HEADS = 16
DH = 64


class Ctx:

    def __init__(self, nc):
        self.nc = nc
        self.es = contextlib.ExitStack()
        self.sem_t = self.es.enter_context(nc.semaphore("sem_t"))
        self.sem_v = self.es.enter_context(nc.semaphore("sem_v"))
        self.sem_s = self.es.enter_context(nc.semaphore("sem_s"))
        self.sem_ld = self.es.enter_context(nc.semaphore("sem_ld"))
        self.sem_ld2 = self.es.enter_context(nc.semaphore("sem_ld2"))
        self.sem_st = self.es.enter_context(nc.semaphore("sem_st"))
        self.t = 0
        self.v = 0
        self.s = 0
        self.ld = 0
        self.ld2 = 0
        self.st = 0

    def barrier(self, block):
        counts = [(self.sem_t, self.t), (self.sem_v, self.v),
                  (self.sem_s, self.s), (self.sem_ld, self.ld),
                  (self.sem_ld2, self.ld2), (self.sem_st, self.st)]
        for engname in ["tensor", "vector", "scalar", "gpsimd", "sync"]:
            @getattr(block, engname)
            def _(eng, counts=counts):
                for sem, val in counts:
                    eng.wait_ge(sem, val)

    def ti(self, instr):
        self.t += 1
        instr.then_inc(self.sem_t, 1)

    def vi(self, instr):
        self.v += 1
        instr.then_inc(self.sem_v, 1)

    def si(self, instr):
        self.s += 1
        instr.then_inc(self.sem_s, 1)


_ARENA = None


def sb(nc, name, shape, dtype):
    if _ARENA is None:
        return nc.alloc_sbuf_tensor(name, shape, dtype).ap()
    return _ARENA.enter_context(nc.sbuf_tensor(name, shape, dtype)).ap()


def ps(nc, name, shape, dtype):
    if _ARENA is None:
        return nc.alloc_psum_tensor(name, shape, dtype).ap()
    return _ARENA.enter_context(nc.psum_tensor(name, shape, dtype)).ap()


def build_a1(fused_nc=None, store=None):
    nc = fused_nc or bass.Bass("TRN2", target_bir_lowering=False)
    ext = "ExternalInput"
    x_d = nc.dram_tensor("x", [T_FULL, D], F32, kind=ext).ap()
    wqT_d = nc.dram_tensor("wqT", [D, H], F32, kind=ext).ap()
    wkT_d = nc.dram_tensor("wkT", [D, H], F32, kind=ext).ap()
    wvT_d = nc.dram_tensor("wvT", [D, H], F32, kind=ext).ap()
    woT_d = nc.dram_tensor("woT", [H, D], F32, kind=ext).ap()
    bq_d = nc.dram_tensor("bq_b", [128, H], F32, kind=ext).ap()
    bk_d = nc.dram_tensor("bk_b", [128, H], F32, kind=ext).ap()
    bv_d = nc.dram_tensor("bv_b", [128, H], F32, kind=ext).ap()
    bo_d = nc.dram_tensor("bo_b", [128, D], F32, kind=ext).ap()
    idf_d = nc.dram_tensor("idf", [128, 128], F32, kind=ext).ap()
    idb_d = nc.dram_tensor("idb", [128, 128], BF16, kind=ext).ap()
    msl_d = nc.dram_tensor("msl", [128, 128], F32, kind=ext).ap()
    msu_d = nc.dram_tensor("msu", [128, 128], F32, kind=ext).ap()
    mui_d = nc.dram_tensor("mui", [128, 128], F32, kind=ext).ap()
    okind = "Internal" if fused_nc else "ExternalOutput"
    xnT_d = nc.dram_tensor("xnT", [8, 128, T_FULL], BF16, kind=okind).ap()
    x1p_d = nc.dram_tensor("x1p", [T_FULL, D], F32, kind=okind).ap()
    if store is not None:
        store["xnT"] = xnT_d
        store["x1p"] = x1p_d

    cx = Ctx(nc) if store is None or "cx" not in store else store["cx"]
    if store is not None:
        store["cx"] = cx
    _st0, _ld20 = cx.st, cx.ld2
    xin = [sb(nc, "xin0", [128, D], F32), sb(nc, "xin1", [128, D], F32)]
    tmpw = sb(nc, "tmpw", [128, D], F32)
    xm = sb(nc, "xm", [128, D], F32)
    xn_bf = sb(nc, "xn_bf", [128, D], BF16)
    xnT_sb = sb(nc, "xnT_sb", [128, D], BF16)
    msum = sb(nc, "msum", [128, 1], F32)
    meanc = sb(nc, "meanc", [128, 1], F32)
    vsum = sb(nc, "vsum", [128, 1], F32)
    rstd = sb(nc, "rstd", [128, 1], F32)
    junk = sb(nc, "junk", [128, 1], F32)
    wq_sb = sb(nc, "wq_sb", [128, D], BF16)
    wk_sb = sb(nc, "wk_sb", [128, D], BF16)
    wv_sb = sb(nc, "wv_sb", [128, D], BF16)
    wo_sb = sb(nc, "wo_sb", [128, D], BF16)
    wldf = sb(nc, "wldf", [128, D], F32)
    bq_b = sb(nc, "bq_s", [128, H], F32)
    bk_b = sb(nc, "bk_s", [128, H], F32)
    bv_b = sb(nc, "bv_s", [128, H], F32)
    bo_b = sb(nc, "bo_s", [128, D], F32)
    ident_f = sb(nc, "ident_f", [128, 128], F32)
    ident_b = sb(nc, "ident_b", [128, 128], BF16)
    mask_sl = sb(nc, "mask_sl", [128, 128], F32)
    mask_su = sb(nc, "mask_su", [128, 128], F32)
    mask_ui = sb(nc, "mask_ui", [128, 128], F32)
    q_f = sb(nc, "q_f", [128, H], F32)
    k_f = sb(nc, "k_f", [128, H], F32)
    v_f = sb(nc, "v_f", [128, H], F32)
    s_col = sb(nc, "s_col", [128, 1], F32)
    kn = sb(nc, "kn", [128, 1], F32)
    uscale = sb(nc, "uscale", [128, 1], F32)
    u_tok = sb(nc, "u_tok", [128, 128], F32)
    u_tok_b = sb(nc, "u_tok_b", [128, 128], BF16)
    q_tok_b = sb(nc, "q_tok_b", [128, 128], BF16)
    v_tok_b = sb(nc, "v_tok_b", [128, 128], BF16)
    Uf = sb(nc, "Uf", [128, 128], BF16)
    Qf = sb(nc, "Qf", [128, 128], BF16)
    A_sb = sb(nc, "A_sb", [128, 128], BF16)
    MT_sb = sb(nc, "MT_sb", [128, 128], BF16)
    A_f32 = sb(nc, "A_f32", [128, 128], F32)
    MT_f32 = sb(nc, "MT_f32", [128, 128], F32)
    W_sb = sb(nc, "W_sb", [128, 128], BF16)
    WT_sb = sb(nc, "WT_sb", [128, 128], BF16)
    S_T = sb(nc, "S_T", [128, 128], F32)
    Cp = sb(nc, "Cp", [128, 128], F32)
    RD = sb(nc, "RD", [128, 128], F32)
    rd_col = sb(nc, "rd_col", [128, 1], F32)
    srd_col = sb(nc, "srd_col", [128, 1], F32)
    CT = sb(nc, "CT", [128, 128], F32)
    N_bf = sb(nc, "N_bf", [128, 128], BF16)
    NT_bf = sb(nc, "NT_bf", [128, 128], BF16)
    P_bf = sb(nc, "P_bf", [128, 128], BF16)
    PT_bf = sb(nc, "PT_bf", [128, 128], BF16)
    Xf_bf = sb(nc, "Xf_bf", [128, 128], BF16)
    Xr_f = sb(nc, "Xr_f", [128, 128], F32)
    Xr_bf = sb(nc, "Xr_bf", [128, 128], BF16)
    Zneg_bf = sb(nc, "Zneg_bf", [128, 128], BF16)
    Alpha_bf = sb(nc, "Alpha_bf", [128, 128], BF16)
    coef_bf = sb(nc, "coef_bf", [128, 128], BF16)
    O_bf = sb(nc, "O_bf", [128, 128], BF16)
    Of_bf = sb(nc, "Of_bf", [128, 128], BF16)
    x1p_sb = [sb(nc, "x1p0", [128, D], F32), sb(nc, "x1p1", [128, D], F32)]
    tmp128 = sb(nc, "tmp128", [128, 128], F32)

    P_row0 = ps(nc, "P_row0", [128, 512], F32)
    P_row1 = ps(nc, "P_row1", [128, 512], F32)
    P_a = ps(nc, "P_a", [128, 512], F32)
    P_b = ps(nc, "P_b", [128, 512], F32)
    P_c = ps(nc, "P_c", [128, 512], F32)
    P_bfp = ps(nc, "P_bfp", [128, 1024], BF16)
    P_wide = ps(nc, "P_wide", [128, 1024], F32)
    rows = [P_row0[:, 0:128], P_row1[:, 0:128]]
    Pa = P_a[:, 0:128]
    Pb = P_b[:, 0:128]
    Pc = P_c[:, 0:128]

    with nc.Block() as block:
        if fused_nc is not None:
            cx.barrier(block)
        @block.sync
        def _(sync):
            for (dst, src) in [(ident_f, idf_d), (ident_b, idb_d),
                               (mask_sl, msl_d), (mask_su, msu_d),
                               (mask_ui, mui_d), (bq_b, bq_d), (bk_b, bk_d),
                               (bv_b, bv_d), (bo_b, bo_d)]:
                sync.dma_start(out=dst[:], in_=src).then_inc(cx.sem_ld, 16)
                cx.ld += 16

        for wsb, wd, dd in [(wq_sb, wqT_d, H), (wk_sb, wkT_d, H),
                            (wv_sb, wvT_d, H), (wo_sb, woT_d, D)]:
            @block.sync
            def _(sync, wd=wd, dd=dd):
                sync.wait_ge(cx.sem_v, cx.v)
                if dd == H:
                    sync.dma_start(
                        out=wldf.rearrange("p (s h) -> p s h", s=8),
                        in_=wd.rearrange("(s p) h -> p s h", p=128),
                    ).then_inc(cx.sem_ld, 16)
                else:
                    sync.dma_start(out=wldf[:], in_=wd).then_inc(cx.sem_ld, 16)
                cx.ld += 16

            @block.vector
            def _(vector, wsb=wsb):
                vector.wait_ge(cx.sem_ld, cx.ld)
                cx.vi(vector.tensor_copy(wsb[:], wldf[:]))

        @block.vector
        def _(vector):
            vector.wait_ge(cx.sem_ld, cx.ld)
            vector.memset(MT_sb[:], 0.0)
            vector.memset(MT_f32[:], 0.0)
            vector.tensor_copy(A_f32[:], ident_f[:])
            cx.vi(vector.tensor_copy(A_sb[:], ident_b[:]))

        @block.sync
        def _(sync):
            sync.dma_start(out=xin[0][:], in_=x_d[0:128, :]).then_inc(cx.sem_ld2, 16)
            cx.ld2 += 16

        for c in range(NCH):
            buf = c % 2
            x_sb = xin[buf]

            if c + 1 < NCH:
                @block.sync
                def _(sync, c=c, buf=buf):
                    if c >= 1:
                        sync.wait_ge(cx.sem_st, _st0 + 32 * c)
                    sync.dma_start(
                        out=xin[1 - buf][:],
                        in_=x_d[(c + 1) * 128:(c + 2) * 128, :],
                    ).then_inc(cx.sem_ld2, 16)
                    cx.ld2 += 16

            @block.vector
            def _(vector, c=c, x_sb=x_sb):
                vector.wait_ge(cx.sem_ld2, _ld20 + 16 * (c + 1))
                if c > 0:
                    vector.wait_ge(cx.sem_t, cx.t)
                vector.reduce_sum(msum[:], x_sb[:], axis=AX.X)
                vector.tensor_copy(junk[:], msum[:])
                vector.tensor_scalar_mul(meanc[:], msum[:], 1.0 / D)
                vector.tensor_copy(junk[:], meanc[:])
                vector.tensor_scalar(xm[:], x_sb[:], meanc[:], None, ALU.subtract)
                vector.tensor_tensor(tmpw[:], xm[:], xm[:], ALU.mult)
                vector.reduce_sum(vsum[:], tmpw[:], axis=AX.X)
                vector.tensor_copy(junk[:], meanc[:])
                cx.vi(vector.tensor_scalar(rstd[:], vsum[:], 1.0 / D, EPS_LN,
                                           ALU.mult, ALU.add))

            @block.scalar
            def _(scalar):
                scalar.wait_ge(cx.sem_v, cx.v)
                cx.si(scalar.sqrt(rstd[:], rstd[:]))

            @block.vector
            def _(vector, c=c):
                vector.wait_ge(cx.sem_s, cx.s)
                vector.engine_nop()
                vector.reciprocal(rstd[:], rstd[:])
                vector.tensor_copy(junk[:], meanc[:])
                vector.tensor_scalar_mul(tmpw[:], xm[:], rstd[:])
                cx.vi(vector.tensor_copy(xn_bf[:], tmpw[:]))

            @block.tensor
            def _(tensor, c=c):
                tensor.wait_ge(cx.sem_v, cx.v)
                for ds in range(8):
                    cx.ti(tensor.transpose(
                        P_bfp[:, ds * 128:(ds + 1) * 128],
                        xn_bf[:, ds * 128:(ds + 1) * 128], ident_b[:]))

            @block.vector
            def _(vector, c=c):
                vector.wait_ge(cx.sem_t, cx.t)
                if c >= 2:
                    vector.wait_ge(cx.sem_st, _st0 + 32 * (c - 1) + 16)
                cx.vi(vector.tensor_copy(xnT_sb[:], P_bfp[:, 0:1024]))

            @block.gpsimd
            def _(gpsimd, c=c):
                gpsimd.wait_ge(cx.sem_v, cx.v)
                gpsimd.dma_start(
                    out=xnT_d[:, :, c * 128:(c + 1) * 128].rearrange(
                        "s p t -> p s t"),
                    in_=xnT_sb.rearrange("p (s t) -> p s t", s=8),
                ).then_inc(cx.sem_st, 16)
                cx.st += 16

            @block.tensor
            def _(tensor, c=c):
                tensor.wait_ge(cx.sem_v, cx.v)
                for (dstp, w) in [(Pa, wq_sb), (Pb, wk_sb), (Pc, wv_sb)]:
                    for ds in range(8):
                        cx.ti(tensor.matmul(
                            dstp, xnT_sb[:, ds * 128:(ds + 1) * 128],
                            w[:, ds * 128:(ds + 1) * 128],
                            start=(ds == 0), stop=(ds == 7)))

            @block.vector
            def _(vector, c=c):
                vector.wait_ge(cx.sem_t, cx.t)
                vector.tensor_tensor(q_f[:], Pa, bq_b[:], ALU.add)
                vector.tensor_tensor(k_f[:], Pb, bk_b[:], ALU.add)
                vector.tensor_tensor(v_f[:], Pc, bv_b[:], ALU.add)
                vector.tensor_tensor(tmp128[:], q_f[:], k_f[:], ALU.mult)
                vector.reduce_sum(s_col[:], tmp128[:], axis=AX.X)
                vector.tensor_tensor(tmp128[:], k_f[:], k_f[:], ALU.mult)
                cx.vi(vector.reduce_sum(kn[:], tmp128[:], axis=AX.X))

            @block.scalar
            def _(scalar):
                scalar.wait_ge(cx.sem_v, cx.v)
                cx.si(scalar.sqrt(uscale[:], kn[:]))

            @block.vector
            def _(vector, c=c):
                vector.wait_ge(cx.sem_s, cx.s)
                vector.tensor_scalar_add(uscale[:], uscale[:], EPS_SCAN)
                vector.engine_nop()
                vector.reciprocal(uscale[:], uscale[:])
                vector.tensor_copy(q_tok_b[:], q_f[:])
                vector.tensor_copy(v_tok_b[:], v_f[:])
                vector.tensor_scalar_mul(u_tok[:], k_f[:], uscale[:])
                cx.vi(vector.tensor_copy(u_tok_b[:], u_tok[:]))

            emit_scan_chunk(
                nc, block, cx, c,
                dict(u_tok_b=u_tok_b, q_tok_b=q_tok_b, v_tok_b=v_tok_b,
                     Uf=Uf, Qf=Qf, A_sb=A_sb, MT_sb=MT_sb, A_f32=A_f32,
                     MT_f32=MT_f32, W_sb=W_sb, WT_sb=WT_sb, S_T=S_T, Cp=Cp,
                     RD=RD, rd_col=rd_col, srd_col=srd_col, CT=CT, N_bf=N_bf,
                     NT_bf=NT_bf, P_bf=P_bf, PT_bf=PT_bf, Xf_bf=Xf_bf,
                     Xr_f=Xr_f, Xr_bf=Xr_bf, Zneg_bf=Zneg_bf,
                     Alpha_bf=Alpha_bf, coef_bf=coef_bf, O_bf=O_bf,
                     s_col=s_col, tmp128=tmp128, junk=junk,
                     ident_f=ident_f, ident_b=ident_b, mask_sl=mask_sl,
                     mask_su=mask_su, mask_ui=mask_ui),
                rows, Pa, Pb, Pc, P_bfp, last=(c == NCH - 1))

            @block.tensor
            def _(tensor, c=c):
                tensor.wait_ge(cx.sem_v, cx.v)
                cx.ti(tensor.transpose(P_bfp[:, 0:128], O_bf[:], ident_b[:]))

            @block.vector
            def _(vector, c=c):
                vector.wait_ge(cx.sem_t, cx.t)
                cx.vi(vector.tensor_copy(Of_bf[:], P_bfp[:, 0:128]))

            @block.tensor
            def _(tensor, c=c):
                tensor.wait_ge(cx.sem_v, cx.v)
                for half in range(2):
                    cx.ti(tensor.matmul(
                        P_wide[:, half * 512:(half + 1) * 512], Of_bf[:],
                        wo_sb[:, half * 512:(half + 1) * 512],
                        start=True, stop=True))

            @block.vector
            def _(vector, c=c, x_sb=x_sb, buf=buf):
                vector.wait_ge(cx.sem_t, cx.t)
                if c >= 2:
                    vector.wait_ge(cx.sem_st, _st0 + 32 * (c - 1) + 32)
                vector.tensor_tensor(tmpw[:], P_wide[:, 0:1024], bo_b[:], ALU.add)
                cx.vi(vector.tensor_tensor(x1p_sb[buf][:], tmpw[:], x_sb[:], ALU.add))

            @block.gpsimd
            def _(gpsimd, c=c, buf=buf):
                gpsimd.wait_ge(cx.sem_v, cx.v)
                gpsimd.dma_start(
                    out=x1p_d[c * 128:(c + 1) * 128, :],
                    in_=x1p_sb[buf][:]).then_inc(cx.sem_st, 16)
                cx.st += 16

        @block.gpsimd
        def _(gpsimd):
            gpsimd.wait_ge(cx.sem_st, cx.st)
    return nc


def emit_scan_chunk(nc, block, cx, c, tn, rows, Pa, Pb, Pc, P_bfp, last):
    t = tn

    @block.tensor
    def _(tensor, c=c):
        tensor.wait_ge(cx.sem_v, cx.v)
        cx.ti(tensor.transpose(P_bfp[:, 0:128], t["u_tok_b"][:], tn["ident_b"][:]))
        cx.ti(tensor.transpose(P_bfp[:, 128:256], t["q_tok_b"][:], tn["ident_b"][:]))

    @block.vector
    def _(vector, c=c):
        vector.wait_ge(cx.sem_t, cx.t)
        vector.tensor_copy(t["Uf"][:], P_bfp[:, 0:128])
        cx.vi(vector.tensor_copy(t["Qf"][:], P_bfp[:, 128:256]))

    @block.tensor
    def _(tensor, c=c):
        tensor.wait_ge(cx.sem_v, cx.v)
        cx.ti(tensor.matmul(Pa, t["A_sb"][:], t["Uf"][:], start=True, stop=True))

    @block.vector
    def _(vector, c=c):
        vector.wait_ge(cx.sem_t, cx.t)
        cx.vi(vector.tensor_copy(t["W_sb"][:], Pa))

    @block.tensor
    def _(tensor, c=c):
        tensor.wait_ge(cx.sem_v, cx.v)
        cx.ti(tensor.matmul(Pb, t["Uf"][:], t["W_sb"][:], start=True, stop=True))
        cx.ti(tensor.transpose(P_bfp[:, 0:128], t["W_sb"][:], tn["ident_b"][:]))

    @block.vector
    def _(vector, c=c):
        vector.wait_ge(cx.sem_t, cx.t)
        vector.tensor_copy(t["WT_sb"][:], P_bfp[:, 0:128])
        cx.vi(vector.tensor_tensor(t["S_T"][:], Pb, tn["ident_f"][:], ALU.add))

    v_base = cx.v

    @block.tensor
    def _(tensor, c=c, v_base=v_base):
        for p in range(C // 2):
            j = 2 * p
            tensor.wait_ge(cx.sem_v, v_base + p)
            cx.ti(tensor.matmul(
                rows[0][:, 0:C - j],
                tn["ident_f"][:, j:j + 1].to_broadcast([128, 128]),
                t["S_T"][:, j:C], start=True, stop=True))
            cx.ti(tensor.matmul(
                rows[1][:, 0:C - j],
                tn["ident_f"][:, j + 1:j + 2].to_broadcast([128, 128]),
                t["S_T"][:, j:C], start=True, stop=True))

    t_base = cx.t - C

    @block.vector
    def _(vector, c=c, t_base=t_base):
        for p in range(C // 2):
            j = 2 * p
            rowsA, rowsB = rows[0], rows[1]
            bfix = t["tmp128"]
            cfix = t["srd_col"]
            vector.wait_ge(cx.sem_t, t_base + 2 * (p + 1))
            vector.reciprocal(t["RD"][:, j:j + 1], rowsA[:, 0:1])
            vector.tensor_copy(t["junk"][:], t["s_col"][:])
            vector.scalar_tensor_tensor(
                t["Cp"][:, j:j + 1], t["S_T"][:, j:j + 1], -1.0,
                t["RD"][:, j:j + 1], ALU.mult, ALU.mult)
            vector.scalar_tensor_tensor(
                cfix[:], rowsA[:, 1:2], -1.0,
                t["RD"][:, j:j + 1], ALU.mult, ALU.mult)
            vector.scalar_tensor_tensor(
                t["S_T"][:, j + 1:C], rowsA[:, 1:C - j], t["Cp"][:, j:j + 1],
                t["S_T"][:, j + 1:C], ALU.mult, ALU.add)
            vector.tensor_copy(t["CT"][:, 0:C - j - 1], rowsB[:, 1:C - j])
            vector.scalar_tensor_tensor(
                bfix[:, 0:C - j - 1], rowsA[:, 1:C - j], cfix[:],
                t["CT"][:, 0:C - j - 1], ALU.mult, ALU.add)
            vector.engine_nop()
            vector.reciprocal(t["RD"][:, j + 1:j + 2], bfix[:, 0:1])
            vector.tensor_copy(t["junk"][:], t["s_col"][:])
            vector.scalar_tensor_tensor(
                t["Cp"][:, j + 1:j + 2], t["S_T"][:, j + 1:j + 2], -1.0,
                t["RD"][:, j + 1:j + 2], ALU.mult, ALU.mult)
            vector.tensor_copy(t["junk"][:], t["s_col"][:])
            if j + 1 < C - 1:
                instr = vector.scalar_tensor_tensor(
                    t["S_T"][:, j + 2:C], bfix[:, 1:C - j - 1],
                    t["Cp"][:, j + 1:j + 2],
                    t["S_T"][:, j + 2:C], ALU.mult, ALU.add)
            else:
                instr = vector.engine_nop()
            cx.vi(instr)

    @block.vector
    def _(vector, c=c):
        vector.tensor_tensor(t["tmp128"][:], t["RD"][:], tn["ident_f"][:], ALU.mult)
        vector.reduce_sum(t["rd_col"][:], t["tmp128"][:], axis=AX.X)
        vector.scalar_tensor_tensor(
            t["NT_bf"][:], t["Cp"][:], -1.0, tn["mask_sl"][:], ALU.mult, ALU.mult)
        cx.vi(vector.tensor_tensor(t["srd_col"][:], t["s_col"][:], t["rd_col"][:], ALU.mult))

    @block.tensor
    def _(tensor, c=c):
        tensor.wait_ge(cx.sem_v, cx.v)
        cx.ti(tensor.transpose(Pa[0:64, 0:128], t["Cp"][:, 0:64], tn["ident_f"][:]))
        cx.ti(tensor.transpose(Pb[0:64, 0:128], t["Cp"][:, 64:128], tn["ident_f"][:]))

    @block.vector
    def _(vector, c=c):
        vector.wait_ge(cx.sem_t, cx.t)
        vector.tensor_copy(t["CT"][0:64, :], Pa[0:64, 0:128])
        vector.tensor_copy(t["CT"][64:128, :], Pb[0:64, 0:128])
        vector.scalar_tensor_tensor(
            t["N_bf"][:], t["CT"][:], -1.0, tn["mask_su"][:], ALU.mult, ALU.mult)
        vector.tensor_tensor(t["tmp128"][:], t["CT"][:], tn["mask_su"][:], ALU.mult)
        vector.tensor_tensor(t["P_bf"][:], t["tmp128"][:], tn["ident_f"][:], ALU.add)
        vector.tensor_tensor(t["tmp128"][:], t["Cp"][:], tn["mask_sl"][:], ALU.mult)
        cx.vi(vector.tensor_tensor(t["PT_bf"][:], t["tmp128"][:], tn["ident_f"][:], ALU.add))

    for lvl in range(6):
        @block.tensor
        def _(tensor, c=c, lvl=lvl):
            tensor.wait_ge(cx.sem_v, cx.v)
            cx.ti(tensor.matmul(Pa, t["NT_bf"][:], t["N_bf"][:], start=True, stop=True))
            cx.ti(tensor.matmul(Pb, t["N_bf"][:], t["NT_bf"][:], start=True, stop=True))

        @block.vector
        def _(vector, c=c, lvl=lvl):
            vector.wait_ge(cx.sem_t, cx.t)
            vector.tensor_copy(t["N_bf"][:], Pa)
            cx.vi(vector.tensor_copy(t["NT_bf"][:], Pb))

        @block.tensor
        def _(tensor, c=c, lvl=lvl):
            tensor.wait_ge(cx.sem_v, cx.v)
            cx.ti(tensor.matmul(Pa, t["PT_bf"][:], t["N_bf"][:], start=True, stop=True))
            if lvl < 5:
                cx.ti(tensor.matmul(Pb, t["N_bf"][:], t["PT_bf"][:], start=True, stop=True))

        @block.vector
        def _(vector, c=c, lvl=lvl):
            vector.wait_ge(cx.sem_t, cx.t)
            vector.tensor_tensor(t["P_bf"][:], t["P_bf"][:], Pa, ALU.add)
            if lvl < 5:
                vector.tensor_tensor(t["PT_bf"][:], t["PT_bf"][:], Pb, ALU.add)
            cx.vi(vector.engine_nop())

    @block.tensor
    def _(tensor, c=c):
        tensor.wait_ge(cx.sem_v, cx.v)
        cx.ti(tensor.matmul(Pa, t["WT_sb"][:], t["P_bf"][:], start=True, stop=True))

    @block.vector
    def _(vector, c=c):
        vector.wait_ge(cx.sem_t, cx.t)
        cx.vi(vector.tensor_copy(t["Xf_bf"][:], Pa))

    @block.tensor
    def _(tensor, c=c):
        tensor.wait_ge(cx.sem_v, cx.v)
        cx.ti(tensor.transpose(P_bfp[:, 0:128], t["Xf_bf"][:], tn["ident_b"][:]))
        cx.ti(tensor.matmul(Pc, t["Xf_bf"][:], t["Qf"][:], start=True, stop=True))

    @block.vector
    def _(vector, c=c):
        vector.wait_ge(cx.sem_t, cx.t)
        vector.tensor_copy(t["Xr_f"][:], P_bfp[:, 0:128])
        vector.tensor_copy(t["Xr_bf"][:], P_bfp[:, 0:128])
        vector.tensor_scalar(t["Zneg_bf"][:], t["Xr_f"][:], t["rd_col"][:], -1.0,
                             ALU.mult, ALU.mult)
        vector.tensor_scalar_mul(t["Alpha_bf"][:], t["Xr_f"][:], t["srd_col"][:])
        cx.vi(vector.scalar_tensor_tensor(
            t["coef_bf"][:], Pc, t["srd_col"][:], tn["mask_ui"][:],
            ALU.mult, ALU.mult))

    @block.tensor
    def _(tensor, c=c):
        tensor.wait_ge(cx.sem_v, cx.v)
        cx.ti(tensor.matmul(Pa, t["Qf"][:], t["MT_sb"][:], start=True, stop=False))
        cx.ti(tensor.matmul(Pa, t["coef_bf"][:], t["v_tok_b"][:], start=False, stop=True))
        cx.ti(tensor.matmul(Pb, t["Alpha_bf"][:], t["v_tok_b"][:], start=True, stop=True))
        cx.ti(tensor.matmul(Pc, t["Zneg_bf"][:], t["Xr_bf"][:], start=True, stop=True))

    @block.vector
    def _(vector, c=c):
        vector.wait_ge(cx.sem_t, cx.t)
        vector.tensor_copy(t["O_bf"][:], Pa)
        vector.tensor_tensor(t["MT_f32"][:], t["MT_f32"][:], Pb, ALU.add)
        vector.tensor_tensor(t["A_f32"][:], t["A_f32"][:], Pc, ALU.add)
        vector.tensor_copy(t["MT_sb"][:], t["MT_f32"][:])
        cx.vi(vector.tensor_copy(t["A_sb"][:], t["A_f32"][:]))



TB0 = 1024
NB0 = T_FULL // TB0


def build_a2(fused_nc=None, store=None, debug=False):
    nc = fused_nc or bass.Bass("TRN2", target_bir_lowering=False)
    ext = "ExternalInput"
    if debug:
        wdump_d = nc.dram_tensor("wdump", [NB0, 128, 8 * NE], F32,
                                 kind="ExternalOutput").ap()
        gdump_d = nc.dram_tensor("gdump", [NB0, 128, 8 * (3 * NE + 3)], F32,
                                 kind="ExternalOutput").ap()
    if store is None:
        xnT_d = nc.dram_tensor("xnT", [8, 128, T_FULL], BF16, kind=ext).ap()
        x1p_d = nc.dram_tensor("x1p", [T_FULL, D], F32, kind=ext).ap()
    else:
        xnT_d = store["xnT"]
        x1p_d = store["x1p"]
    gw_d = nc.dram_tensor("gw", [D, NE], F32, kind=ext).ap()
    w1_d = nc.dram_tensor("w1", [NE, D, ECAP], F32, kind=ext).ap()
    b1_d = nc.dram_tensor("b1l", [128, NE * 4], F32, kind=ext).ap()
    w2_d = nc.dram_tensor("w2", [NE, ECAP, D], F32, kind=ext).ap()
    b2_d = nc.dram_tensor("b2l", [NE, D], F32, kind=ext).ap()
    idb_d = nc.dram_tensor("idb2", [128, 128], BF16, kind=ext).ap()
    okind = "Internal" if fused_nc else "ExternalOutput"
    x1_d = nc.dram_tensor("x1", [T_FULL, D], F32, kind=okind).ap()
    if store is not None:
        store["x1"] = x1_d

    cx = Ctx(nc) if fused_nc is None else store["cx"]
    _st0, _ld20 = cx.st, cx.ld2
    NT = 8

    w1_sb = sb(nc, "a2_w1", [128, NE * 8 * ECAP], BF16)
    w2_sb = sb(nc, "a2_w2", [128, NE * 4 * D], BF16)
    gw_sb = sb(nc, "a2_gw", [128, 8 * NE], BF16)
    b1_sb = sb(nc, "a2_b1", [128, NE * 4], F32)
    b2_sb = sb(nc, "a2_b2", [NE, D], BF16)
    ident_b = sb(nc, "a2_idb", [128, 128], BF16)
    xnT_blk = sb(nc, "a2_xnT", [128, 8 * TB0], BF16)
    wld = xnT_blk.bitcast(F32)
    acc = sb(nc, "a2_acc", [128, 8 * D], F32)
    h_sb = sb(nc, "a2_h", [128, 4 * TB0], BF16)
    gl_all = sb(nc, "a2_glall", [128, NT * NE], F32)
    ex_all = sb(nc, "a2_exall", [128, NT * NE], F32)
    msk_all = sb(nc, "a2_mskall", [128, NT * NE], F32)
    mskd_all = sb(nc, "a2_mskdall", [128, NT * NE], F32)
    selt_all = sb(nc, "a2_seltall", [128, NT * NE], F32)
    sel2 = sb(nc, "a2_sel2", [128, NT * NE], F32)
    m1L = sb(nc, "a2_m1L", [128, NT], F32)
    m2L = sb(nc, "a2_m2L", [128, NT], F32)
    ssum8 = sb(nc, "a2_ssum8", [128, NT], F32)
    num8 = sb(nc, "a2_num8", [128, NT], F32)
    rs8 = sb(nc, "a2_rs8", [128, NT], F32)
    tokw8 = sb(nc, "a2_tokw8", [128, NT], F32)
    wsel = sb(nc, "a2_wsel", [128, NT * NE], F32)
    if debug:
        gdbg = sb(nc, "a2_gdbg", [128, 8 * (3 * NE + 3)], F32)
    wsel_bf_all = sb(nc, "a2_wselb", [128, NT * NE], BF16)
    wselT = sb(nc, "a2_wselT", [NE, 128], BF16)
    junk = sb(nc, "a2_junk", [128, 1], F32)
    junk2 = sb(nc, "a2_junk2", [128, 1], F32)

    P_h = ps(nc, "a2_Ph", [128, 512], F32)
    P_y = ps(nc, "a2_Py", [128, 512], F32)
    P_g = ps(nc, "a2_Pg", [128, 512], F32)
    P_t = ps(nc, "a2_Pt", [128, 1024], F32)
    P_bf2 = ps(nc, "a2_Pbf", [128, 1024], BF16)

    with nc.Block() as block:
        if fused_nc is not None:
            cx.barrier(block)
        @block.sync
        def _(sync):
            sync.dma_start(out=ident_b[:], in_=idb_d).then_inc(cx.sem_ld, 16)
            sync.dma_start(out=b1_sb[:], in_=b1_d).then_inc(cx.sem_ld, 16)
            cx.ld += 32

        @block.sync
        def _(sync):
            sync.dma_start(
                out=wld[:, 0:64].rearrange("p (s g) -> p s g", s=8),
                in_=gw_d.rearrange("(s p) g -> p s g", p=128),
            ).then_inc(cx.sem_ld, 16)
            cx.ld += 16

        @block.vector
        def _(vector):
            vector.wait_ge(cx.sem_ld, cx.ld)
            cx.vi(vector.tensor_copy(gw_sb[:], wld[:, 0:64]))

        @block.sync
        def _(sync):
            sync.wait_ge(cx.sem_v, cx.v)
            sync.dma_start(out=wld[0:NE, 0:D], in_=b2_d).then_inc(cx.sem_ld, 16)
            cx.ld += 16

        @block.vector
        def _(vector):
            vector.wait_ge(cx.sem_ld, cx.ld)
            cx.vi(vector.tensor_copy(b2_sb[:], wld[0:NE, 0:D]))

        for e in range(NE):
            @block.sync
            def _(sync, e=e):
                sync.wait_ge(cx.sem_v, cx.v)
                sync.dma_start(
                    out=wld.rearrange("p (s f) -> p s f", s=8),
                    in_=w1_d[e].rearrange("(s p) f -> p s f", p=128),
                ).then_inc(cx.sem_ld, 16)
                cx.ld += 16

            @block.vector
            def _(vector, e=e):
                vector.wait_ge(cx.sem_ld, cx.ld)
                cx.vi(vector.tensor_copy(w1_sb[:, e * 4096:(e + 1) * 4096], wld[:]))

        for e in range(NE):
            @block.sync
            def _(sync, e=e):
                sync.wait_ge(cx.sem_v, cx.v)
                sync.dma_start(
                    out=wld.rearrange("p (s d) -> p s d", s=4),
                    in_=w2_d[e].rearrange("(s p) d -> p s d", p=128),
                ).then_inc(cx.sem_ld, 16)
                cx.ld += 16

            @block.vector
            def _(vector, e=e):
                vector.wait_ge(cx.sem_ld, cx.ld)
                cx.vi(vector.tensor_copy(w2_sb[:, e * 4096:(e + 1) * 4096], wld[:]))

        st_per_blk = 16
        for b in range(NB0):
            @block.sync
            def _(sync, b=b):
                if b > 0:
                    sync.wait_ge(cx.sem_st, cx.st)
                else:
                    sync.wait_ge(cx.sem_v, cx.v)
                sync.dma_start(
                    out=xnT_blk.rearrange("p (s t) -> p s t", s=8),
                    in_=xnT_d[:, :, b * TB0:(b + 1) * TB0].rearrange("s p t -> p s t"),
                ).then_inc(cx.sem_ld2, 16)
                sync.dma_start(
                    out=acc.rearrange("p (tt d) -> p tt d", tt=8),
                    in_=x1p_d[b * TB0:(b + 1) * TB0, :].rearrange("(tt p) d -> p tt d", p=128),
                ).then_inc(cx.sem_ld2, 16)
                cx.ld2 += 32

            @block.tensor
            def _(tensor, b=b):
                tensor.wait_ge(cx.sem_ld2, cx.ld2)
                tensor.wait_ge(cx.sem_v, cx.v)
                for tt in range(NT):
                    for ds in range(8):
                        cx.ti(tensor.matmul(
                            P_g[:, tt * NE:(tt + 1) * NE],
                            xnT_blk[:, ds * TB0 + tt * 128: ds * TB0 + (tt + 1) * 128],
                            gw_sb[:, ds * NE:(ds + 1) * NE],
                            start=(ds == 0), stop=(ds == 7)))

            @block.vector
            def _(vector, b=b):
                vector.wait_ge(cx.sem_t, cx.t)
                vector.tensor_copy(gl_all[:], P_g[:, 0:NT * NE])
                vector.tensor_copy(junk[:], junk2[:])
                vector.tensor_copy(junk2[:], junk[:])
                cx.vi(vector.tensor_copy(junk[:], junk2[:]))

            @block.scalar
            def _(scalar, b=b):
                scalar.wait_ge(cx.sem_v, cx.v)
                cx.si(scalar.activation(ex_all[:], gl_all[:], AF.Exp))

            @block.vector
            def _(vector, b=b):
                vector.wait_ge(cx.sem_s, cx.s)
                for tt in range(NT):
                    vector.reduce_max(m1L[:, tt:tt + 1],
                                      gl_all[:, tt * NE:(tt + 1) * NE], axis=AX.X)
                for tt in range(NT):
                    vector.tensor_scalar(msk_all[:, tt * NE:(tt + 1) * NE],
                                         gl_all[:, tt * NE:(tt + 1) * NE],
                                         m1L[:, tt:tt + 1], None, ALU.is_equal)
                for tt in range(NT):
                    vector.scalar_tensor_tensor(
                        mskd_all[:, tt * NE:(tt + 1) * NE],
                        msk_all[:, tt * NE:(tt + 1) * NE], -1e30,
                        gl_all[:, tt * NE:(tt + 1) * NE], ALU.mult, ALU.add)
                for tt in range(NT):
                    vector.reduce_max(m2L[:, tt:tt + 1],
                                      mskd_all[:, tt * NE:(tt + 1) * NE], axis=AX.X)
                for tt in range(NT):
                    vector.tensor_scalar(selt_all[:, tt * NE:(tt + 1) * NE],
                                         gl_all[:, tt * NE:(tt + 1) * NE],
                                         m2L[:, tt:tt + 1], None, ALU.is_ge)
                for tt in range(NT):
                    vector.reduce_sum(ssum8[:, tt:tt + 1],
                                      ex_all[:, tt * NE:(tt + 1) * NE], axis=AX.X)
                vector.tensor_tensor(sel2[:], selt_all[:], ex_all[:], ALU.mult)
                vector.tensor_copy(junk[:], junk2[:])
                vector.tensor_copy(junk2[:], junk[:])
                vector.reciprocal(rs8[:], ssum8[:])
                for tt in range(NT):
                    vector.reduce_sum(num8[:, tt:tt + 1],
                                      sel2[:, tt * NE:(tt + 1) * NE], axis=AX.X)
                vector.tensor_copy(junk[:], junk2[:])
                vector.tensor_copy(junk2[:], junk[:])
                vector.tensor_copy(junk[:], junk2[:])
                vector.tensor_tensor(tokw8[:], num8[:], rs8[:], ALU.mult)
                vector.tensor_copy(junk[:], junk2[:])
                vector.tensor_copy(junk2[:], junk[:])
                vector.tensor_copy(junk[:], junk2[:])
                for tt in range(NT):
                    vector.tensor_scalar_mul(wsel[:, tt * NE:(tt + 1) * NE],
                                             selt_all[:, tt * NE:(tt + 1) * NE],
                                             tokw8[:, tt:tt + 1])
                vector.tensor_copy(junk[:], junk2[:])
                vector.tensor_copy(junk2[:], junk[:])
                vector.tensor_copy(junk[:], junk2[:])
                vector.tensor_copy(wsel_bf_all[:], wsel[:, 0:NT * NE])
                if debug:
                    vector.tensor_copy(gdbg[:, 0:64], gl_all[:])
                    vector.tensor_copy(gdbg[:, 64:128], ex_all[:])
                    vector.tensor_copy(gdbg[:, 128:136], m1L[:])
                    vector.tensor_copy(gdbg[:, 136:144], m2L[:])
                    vector.tensor_copy(gdbg[:, 144:152], ssum8[:])
                    vector.tensor_copy(gdbg[:, 152:160], tokw8[:])
                vector.tensor_copy(junk2[:], junk[:])
                cx.vi(vector.tensor_copy(junk[:], junk2[:]))

            for tt in range(NT):
                @block.tensor
                def _(tensor, b=b, tt=tt):
                    tensor.wait_ge(cx.sem_v, cx.v)
                    cx.ti(tensor.transpose(
                        P_bf2[:, 0:128][0:NE, :],
                        wsel_bf_all[:, tt * NE:(tt + 1) * NE], ident_b[:]))

                @block.vector
                def _(vector, b=b, tt=tt):
                    vector.wait_ge(cx.sem_t, cx.t)
                    cx.vi(vector.tensor_copy(wselT[:], P_bf2[0:NE, 0:128]))

                @block.tensor
                def _(tensor, b=b, tt=tt):
                    tensor.wait_ge(cx.sem_v, cx.v)
                    for half in range(2):
                        cx.ti(tensor.matmul(
                            P_t[:, half * 512:(half + 1) * 512], wselT[:],
                            b2_sb[:, half * 512:(half + 1) * 512],
                            start=True, stop=True))

                @block.vector
                def _(vector, b=b, tt=tt):
                    vector.wait_ge(cx.sem_t, cx.t)
                    cx.vi(vector.tensor_tensor(
                        acc[:, tt * D:(tt + 1) * D], acc[:, tt * D:(tt + 1) * D],
                        P_t[:, 0:1024], ALU.add))

            for e in range(NE):
                for ft in range(4):
                    for tc in range(TB0 // 512):
                        @block.tensor
                        def _(tensor, b=b, e=e, ft=ft, tc=tc):
                            tensor.wait_ge(cx.sem_v, cx.v)
                            tensor.wait_ge(cx.sem_s, cx.s)
                            for ds in range(8):
                                cx.ti(tensor.matmul(
                                    P_h[:, 0:512],
                                    w1_sb[:, e * 4096 + ds * 512 + ft * 128:
                                          e * 4096 + ds * 512 + (ft + 1) * 128],
                                    xnT_blk[:, ds * TB0 + tc * 512: ds * TB0 + (tc + 1) * 512],
                                    start=(ds == 0), stop=(ds == 7)))

                        @block.scalar
                        def _(scalar, b=b, e=e, ft=ft, tc=tc):
                            scalar.wait_ge(cx.sem_t, cx.t)
                            cx.si(scalar.activation(
                                h_sb[:, ft * TB0 + tc * 512: ft * TB0 + (tc + 1) * 512],
                                P_h[:, 0:512], AF.Gelu,
                                bias=b1_sb[:, e * 4 + ft: e * 4 + ft + 1]))

                for tt in range(NT):
                    for dc in range(2):
                        @block.tensor
                        def _(tensor, b=b, e=e, tt=tt, dc=dc):
                            tensor.wait_ge(cx.sem_s, cx.s)
                            tensor.wait_ge(cx.sem_v, cx.v)
                            for fs in range(4):
                                cx.ti(tensor.matmul(
                                    P_y[:, 0:512],
                                    h_sb[:, fs * TB0 + tt * 128: fs * TB0 + (tt + 1) * 128],
                                    w2_sb[:, e * 4096 + fs * 1024 + dc * 512:
                                          e * 4096 + fs * 1024 + (dc + 1) * 512],
                                    start=(fs == 0), stop=(fs == 3)))

                        @block.vector
                        def _(vector, b=b, e=e, tt=tt, dc=dc):
                            vector.wait_ge(cx.sem_t, cx.t)
                            dst = acc[:, tt * D + dc * 512: tt * D + (dc + 1) * 512]
                            cx.vi(vector.scalar_tensor_tensor(
                                dst, P_y[:, 0:512], wsel[:, tt * NE + e: tt * NE + e + 1],
                                dst, ALU.mult, ALU.add))

            @block.gpsimd
            def _(gpsimd, b=b):
                gpsimd.wait_ge(cx.sem_v, cx.v)
                gpsimd.dma_start(
                    out=x1_d[b * TB0:(b + 1) * TB0, :].rearrange("(tt p) d -> p tt d", p=128),
                    in_=acc.rearrange("p (tt d) -> p tt d", tt=8),
                ).then_inc(cx.sem_st, 16)
                cx.st += 16
                if debug:
                    gpsimd.dma_start(out=wdump_d[b],
                                     in_=wsel[:]).then_inc(cx.sem_st, 16)
                    gpsimd.dma_start(out=gdump_d[b],
                                     in_=gdbg[:]).then_inc(cx.sem_st, 16)
                    cx.st += 32

        @block.gpsimd
        def _(gpsimd):
            gpsimd.wait_ge(cx.sem_st, cx.st)
    return nc


def build_b1(fused_nc=None, store=None, debug=False):
    nc = fused_nc or bass.Bass("TRN2", target_bir_lowering=False)
    ext = "ExternalInput"
    if debug:
        ksd_d = nc.dram_tensor("ksd", [T_FULL, D], BF16, kind="ExternalOutput").ap()
        vsd_d = nc.dram_tensor("vsd", [T_FULL, D], BF16, kind="ExternalOutput").ap()
    if store is None:
        x1_d = nc.dram_tensor("x1", [T_FULL, D], F32, kind=ext).ap()
    else:
        x1_d = store["x1"]
    wqkv_d = nc.dram_tensor("wqkv", [D, 3 * D], F32, kind=ext).ap()
    qnw_d = nc.dram_tensor("qnw_b", [128, D], F32, kind=ext).ap()
    knw_d = nc.dram_tensor("knw_b", [128, D], F32, kind=ext).ap()
    idb_d = nc.dram_tensor("idb3", [128, 128], BF16, kind=ext).ap()
    ones_d = nc.dram_tensor("ones_c", [128, 1], BF16, kind=ext).ap()
    okind = "Internal" if fused_nc else "ExternalOutput"
    xn1T_d = nc.dram_tensor("xn1T", [8, 128, T_FULL], BF16, kind=okind).ap()
    qs_d = nc.dram_tensor("qs", [T_FULL, D], BF16, kind=okind).ap()
    kv_d = nc.dram_tensor("kv", [128, 512], F32, kind=okind).ap()
    ksum_d = nc.dram_tensor("ksum", [D], F32, kind=okind).ap()
    if store is not None:
        store["xn1T"] = xn1T_d
        store["qs"] = qs_d
        store["kv"] = kv_d
        store["ksum"] = ksum_d

    cx = Ctx(nc) if fused_nc is None else store["cx"]
    _st0, _ld20 = cx.st, cx.ld2

    xin = [sb(nc, "b1_x0", [128, D], F32), sb(nc, "b1_x1", [128, D], F32)]
    tmpw = sb(nc, "b1_tmpw", [128, D], F32)
    xm = sb(nc, "b1_xm", [128, D], F32)
    xn_bf = sb(nc, "b1_xnbf", [128, D], BF16)
    xnT_sb = sb(nc, "b1_xnT", [128, D], BF16)
    msum = sb(nc, "b1_msum", [128, 1], F32)
    meanc = sb(nc, "b1_meanc", [128, 1], F32)
    vsum = sb(nc, "b1_vsum", [128, 1], F32)
    rstd = sb(nc, "b1_rstd", [128, 1], F32)
    junk = sb(nc, "b1_junk", [128, 1], F32)
    wqkv_sb = sb(nc, "b1_wqkv", [128, 8 * 3 * D], BF16)
    wld = sb(nc, "b1_wld", [128, 8192], F32)
    qnw_b = sb(nc, "b1_qnw", [128, D], F32)
    knw_b = sb(nc, "b1_knw", [128, D], F32)
    ident_b = sb(nc, "b1_idb", [128, 128], BF16)
    ones_c = sb(nc, "b1_ones", [128, 1], BF16)
    qs_bf = sb(nc, "b1_qsbf", [128, D], BF16)
    ks_bf = sb(nc, "b1_ksbf", [128, D], BF16)
    vs_bf = sb(nc, "b1_vsbf", [128, D], BF16)
    kv_acc = sb(nc, "b1_kvacc", [128, 512], F32)
    ksum_row = sb(nc, "b1_ksrow", [1, 1024], F32)
    qraw = sb(nc, "b1_qraw", [128, D], F32)
    junk2 = sb(nc, "b1_junk2", [128, 1], F32)

    P_q = ps(nc, "b1_Pq", [128, 1024], F32)
    P_kv = ps(nc, "b1_Pkv", [128, 512], F32)
    P_ks = ps(nc, "b1_Pks", [128, 1024], F32)
    P_bf3 = ps(nc, "b1_Pbf", [128, 1024], BF16)

    with nc.Block() as block:
        if fused_nc is not None:
            cx.barrier(block)
        @block.sync
        def _(sync):
            for (dst, src) in [(qnw_b, qnw_d), (knw_b, knw_d),
                               (ident_b, idb_d), (ones_c, ones_d)]:
                sync.dma_start(out=dst[:], in_=src).then_inc(cx.sem_ld, 16)
                cx.ld += 16

        for part in range(3):
            @block.sync
            def _(sync, part=part):
                sync.wait_ge(cx.sem_v, cx.v)
                sync.dma_start(
                    out=wld.rearrange("p (s n) -> p s n", s=8),
                    in_=wqkv_d[:, part * D:(part + 1) * D].rearrange(
                        "(s p) n -> p s n", p=128),
                ).then_inc(cx.sem_ld, 16)
                cx.ld += 16

            @block.vector
            def _(vector, part=part):
                vector.wait_ge(cx.sem_ld, cx.ld)
                for ds in range(8):
                    instr = vector.tensor_copy(
                        wqkv_sb[:, ds * 3 * D + part * D: ds * 3 * D + (part + 1) * D],
                        wld[:, ds * D:(ds + 1) * D])
                cx.vi(instr)

        @block.vector
        def _(vector):
            vector.memset(kv_acc[:], 0.0)
            cx.vi(vector.memset(ksum_row[:], 0.0))

        @block.sync
        def _(sync):
            sync.dma_start(out=xin[0][:], in_=x1_d[0:128, :]).then_inc(cx.sem_ld2, 16)
            cx.ld2 += 16

        for c in range(NCH):
            buf = c % 2
            x_sb = xin[buf]
            if c + 1 < NCH:
                @block.sync
                def _(sync, c=c, buf=buf):
                    if c >= 1:
                        sync.wait_ge(cx.sem_st, _st0 + 32 * c)
                    sync.dma_start(
                        out=xin[1 - buf][:],
                        in_=x1_d[(c + 1) * 128:(c + 2) * 128, :],
                    ).then_inc(cx.sem_ld2, 16)
                    cx.ld2 += 16

            @block.vector
            def _(vector, c=c, x_sb=x_sb):
                vector.wait_ge(cx.sem_ld2, _ld20 + 16 * (c + 1))
                if c > 0:
                    vector.wait_ge(cx.sem_t, cx.t)
                vector.reduce_sum(msum[:], x_sb[:], axis=AX.X)
                vector.tensor_copy(junk[:], msum[:])
                vector.tensor_scalar_mul(meanc[:], msum[:], 1.0 / D)
                vector.tensor_copy(junk[:], meanc[:])
                vector.tensor_scalar(xm[:], x_sb[:], meanc[:], None, ALU.subtract)
                vector.tensor_tensor(tmpw[:], xm[:], xm[:], ALU.mult)
                vector.reduce_sum(vsum[:], tmpw[:], axis=AX.X)
                vector.tensor_copy(junk[:], meanc[:])
                cx.vi(vector.tensor_scalar(rstd[:], vsum[:], 1.0 / D, EPS_LN,
                                           ALU.mult, ALU.add))

            @block.scalar
            def _(scalar):
                scalar.wait_ge(cx.sem_v, cx.v)
                cx.si(scalar.sqrt(rstd[:], rstd[:]))

            @block.vector
            def _(vector, c=c):
                vector.wait_ge(cx.sem_s, cx.s)
                vector.engine_nop()
                vector.reciprocal(rstd[:], rstd[:])
                vector.tensor_copy(junk[:], meanc[:])
                vector.tensor_scalar_mul(tmpw[:], xm[:], rstd[:])
                cx.vi(vector.tensor_copy(xn_bf[:], tmpw[:]))

            @block.tensor
            def _(tensor, c=c):
                tensor.wait_ge(cx.sem_v, cx.v)
                for ds in range(8):
                    cx.ti(tensor.transpose(
                        P_bf3[:, ds * 128:(ds + 1) * 128],
                        xn_bf[:, ds * 128:(ds + 1) * 128], ident_b[:]))

            @block.vector
            def _(vector, c=c):
                vector.wait_ge(cx.sem_t, cx.t)
                if c >= 2:
                    vector.wait_ge(cx.sem_st, _st0 + 32 * (c - 1) + 16)
                cx.vi(vector.tensor_copy(xnT_sb[:], P_bf3[:, 0:1024]))

            @block.gpsimd
            def _(gpsimd, c=c):
                gpsimd.wait_ge(cx.sem_v, cx.v)
                gpsimd.dma_start(
                    out=xn1T_d[:, :, c * 128:(c + 1) * 128].rearrange("s p t -> p s t"),
                    in_=xnT_sb.rearrange("p (s t) -> p s t", s=8),
                ).then_inc(cx.sem_st, 16)
                cx.st += 16

            for part, (dstb, gain) in enumerate(
                    [(qs_bf, qnw_b), (ks_bf, knw_b), (vs_bf, None)]):
                @block.tensor
                def _(tensor, c=c, part=part):
                    tensor.wait_ge(cx.sem_v, cx.v)
                    for half in range(2):
                        for ds in range(8):
                            cx.ti(tensor.matmul(
                                P_q[:, half * 512:(half + 1) * 512],
                                xnT_sb[:, ds * 128:(ds + 1) * 128],
                                wqkv_sb[:, ds * 3 * D + part * D + half * 512:
                                        ds * 3 * D + part * D + (half + 1) * 512],
                                start=(ds == 0), stop=(ds == 7)))

                @block.vector
                def _(vector, c=c, part=part, dstb=dstb, gain=gain):
                    vector.wait_ge(cx.sem_t, cx.t)
                    if gain is not None:
                        vector.tensor_copy(qraw[:], P_q[:, 0:1024])
                        vector.tensor_copy(junk[:], junk2[:])
                        vector.tensor_copy(junk2[:], junk[:])
                        vector.tensor_copy(junk[:], junk2[:])
                        vector.tensor_tensor(tmpw[:], qraw[:], qraw[:], ALU.mult)
                        vector.reduce_sum(vsum[:], tmpw[:], axis=AX.X)
                        vector.tensor_copy(junk[:], meanc[:])
                        cx.vi(vector.tensor_scalar(rstd[:], vsum[:], 1.0 / D, EPS_RMS,
                                                   ALU.mult, ALU.add))
                    else:
                        cx.vi(vector.tensor_copy(dstb[:], P_q[:, 0:1024]))

                if gain is not None:
                    @block.scalar
                    def _(scalar):
                        scalar.wait_ge(cx.sem_v, cx.v)
                        cx.si(scalar.sqrt(rstd[:], rstd[:]))

                    @block.vector
                    def _(vector, c=c, part=part, dstb=dstb, gain=gain):
                        vector.wait_ge(cx.sem_s, cx.s)
                        vector.engine_nop()
                        vector.reciprocal(rstd[:], rstd[:])
                        vector.tensor_copy(junk[:], meanc[:])
                        vector.tensor_scalar_mul(tmpw[:], qraw[:], rstd[:])
                        vector.tensor_tensor(tmpw[:], tmpw[:], gain[:], ALU.mult)
                        cx.vi(vector.tensor_scalar_max(dstb[:], tmpw[:], 0.0))

            @block.gpsimd
            def _(gpsimd, c=c):
                gpsimd.wait_ge(cx.sem_v, cx.v)
                gpsimd.dma_start(out=qs_d[c * 128:(c + 1) * 128, :],
                                 in_=qs_bf[:]).then_inc(cx.sem_st, 16)
                cx.st += 16
                if debug:
                    gpsimd.dma_start(out=ksd_d[c * 128:(c + 1) * 128, :],
                                     in_=ks_bf[:]).then_inc(cx.sem_ld, 16)
                    gpsimd.dma_start(out=vsd_d[c * 128:(c + 1) * 128, :],
                                     in_=vs_bf[:]).then_inc(cx.sem_ld, 16)
                    cx.ld += 32

            @block.tensor
            def _(tensor, c=c):
                tensor.wait_ge(cx.sem_v, cx.v)
                for h in [0, 2, 4, 6, 8, 10, 12, 14, 1, 3, 5, 7, 9, 11, 13, 15]:
                    po = (h % 2) * 64
                    fo = (h // 2) * 64
                    cx.ti(tensor.matmul(
                        P_kv[po:po + 64, fo:fo + 64],
                        ks_bf[:, h * 64:(h + 1) * 64],
                        vs_bf[:, h * 64:(h + 1) * 64],
                        start=True, stop=True))
                for half in range(2):
                    cx.ti(tensor.matmul(
                        P_ks[0:1, half * 512:(half + 1) * 512],
                        ones_c[:],
                        ks_bf[:, half * 512:(half + 1) * 512],
                        start=True, stop=True))

            @block.vector
            def _(vector, c=c):
                vector.wait_ge(cx.sem_t, cx.t)
                vector.tensor_tensor(kv_acc[:], kv_acc[:], P_kv[:, 0:512], ALU.add)
                cx.vi(vector.tensor_tensor(ksum_row[:], ksum_row[:],
                                           P_ks[0:1, 0:1024], ALU.add))

        @block.gpsimd
        def _(gpsimd):
            gpsimd.wait_ge(cx.sem_v, cx.v)
            gpsimd.dma_start(out=kv_d, in_=kv_acc[:]).then_inc(cx.sem_st, 16)
            gpsimd.dma_start(out=ksum_d.unsqueeze(0),
                             in_=ksum_row[:]).then_inc(cx.sem_st, 16)
            cx.st += 32
            gpsimd.wait_ge(cx.sem_st, cx.st)
    return nc


TB1 = 1024
NB1 = T_FULL // TB1
B2_SERIAL_H = False
B2_SERIAL_Y = False
B2_W3_SYNC = False
FS = 256
NFS = FF // FS


def build_b2(fused_nc=None, store=None):
    nc = fused_nc or bass.Bass("TRN2", target_bir_lowering=False)
    ext = "ExternalInput"
    if store is None:
        xn1T_d = nc.dram_tensor("xn1T", [8, 128, T_FULL], BF16, kind=ext).ap()
        x1_d = nc.dram_tensor("x1", [T_FULL, D], F32, kind=ext).ap()
    else:
        xn1T_d = store["xn1T"]
        x1_d = store["x1"]
    wg_d = nc.dram_tensor("wg", [D, NE], F32, kind=ext).ap()
    ws1_d = nc.dram_tensor("ws1", [D, FF], F32, kind=ext).ap()
    ws3_d = nc.dram_tensor("ws3", [D, FF], F32, kind=ext).ap()
    ws2_d = nc.dram_tensor("ws2", [FF, D], F32, kind=ext).ap()
    we1_d = nc.dram_tensor("we1", [NE, D, FF], F32, kind=ext).ap()
    we3_d = nc.dram_tensor("we3", [NE, D, FF], F32, kind=ext).ap()
    we2_d = nc.dram_tensor("we2", [NE, FF, D], F32, kind=ext).ap()
    okind = "Internal" if fused_nc else "ExternalOutput"
    x2p_d = nc.dram_tensor("x2p", [T_FULL, D], F32, kind=okind).ap()
    if store is not None:
        store["x2p"] = x2p_d

    cx = Ctx(nc) if fused_nc is None else store["cx"]
    _st0, _ld20 = cx.st, cx.ld2
    NT = TB1 // 128

    xnT_blk = sb(nc, "b2_xnT", [128, 8 * TB1], BF16)
    acc = sb(nc, "b2_acc", [128, NT * D], F32)
    h_sb = sb(nc, "b2_h", [128, 16 * TB1], BF16)
    gw_sb = sb(nc, "b2_gw", [128, 8 * NE], BF16)
    wld_g = sb(nc, "b2_wldg", [128, 64], F32)
    w13ld = [sb(nc, "b2_w13a", [128, 2 * 8 * FS], F32),
             sb(nc, "b2_w13b", [128, 2 * 8 * FS], F32)]
    w2ld = [sb(nc, "b2_w2a", [128, 4096], F32),
            sb(nc, "b2_w2b", [128, 4096], F32)]
    wb13 = [sb(nc, "b2_wb13a", [128, 16 * FS], BF16),
            sb(nc, "b2_wb13b", [128, 16 * FS], BF16)]
    wb2 = [sb(nc, "b2_wb2a", [128, 4096], BF16),
           sb(nc, "b2_wb2b", [128, 4096], BF16)]
    sl2 = [sb(nc, "b2_sl2a", [128, 512], BF16),
           sb(nc, "b2_sl2b", [128, 512], BF16)]
    gl_all = sb(nc, "b2_glall", [128, NT * NE], F32)
    g_all = sb(nc, "b2_gall", [128, NT * NE], F32)
    msk_all = sb(nc, "b2_mskall", [128, NT * NE], F32)
    mskd_all = sb(nc, "b2_mskdall", [128, NT * NE], F32)
    selt_all = sb(nc, "b2_seltall", [128, NT * NE], F32)
    sg_all = sb(nc, "b2_sgall", [128, NT * NE], F32)
    m1L = sb(nc, "b2_m1L", [128, NT], F32)
    m2L = sb(nc, "b2_m2L", [128, NT], F32)
    den8 = sb(nc, "b2_den8", [128, NT], F32)
    rd8 = sb(nc, "b2_rd8", [128, NT], F32)
    wsel = sb(nc, "b2_wsel", [128, NT * NE], F32)
    junk = sb(nc, "b2_junk", [128, 1], F32)
    junk2 = sb(nc, "b2_junk2", [128, 1], F32)

    P_h1 = ps(nc, "b2_Ph1", [128, 1024], F32)
    P_h3 = ps(nc, "b2_Ph3", [128, 1024], F32)
    P_y = ps(nc, "b2_Py", [128, 1024], F32)
    P_g = ps(nc, "b2_Pg", [128, 512], F32)

    sem_w1 = [cx.es.enter_context(nc.semaphore("b2_w1e")),
              cx.es.enter_context(nc.semaphore("b2_w1o"))]
    sem_w3 = [cx.es.enter_context(nc.semaphore("b2_w3e")),
              cx.es.enter_context(nc.semaphore("b2_w3o"))]
    sem_w2p = [cx.es.enter_context(nc.semaphore("b2_w2e")),
               cx.es.enter_context(nc.semaphore("b2_w2o"))]
    c_w1 = [0, 0]
    c_w3 = [0, 0]
    c_w2 = [0, 0]

    def w13_src(e, f0, f1):
        if e < 0:
            return ws1_d[:, f0:f1], ws3_d[:, f0:f1]
        return we1_d[e][:, f0:f1], we3_d[e][:, f0:f1]

    def w2_src(e, f0, f1):
        if e < 0:
            return ws2_d[f0:f1, :]
        return we2_d[e][f0:f1, :]

    with nc.Block() as block:
        if fused_nc is not None:
            cx.barrier(block)
        @block.sync
        def _(sync):
            sync.dma_start(
                out=wld_g.rearrange("p (s g) -> p s g", s=8),
                in_=wg_d.rearrange("(s p) g -> p s g", p=128),
            ).then_inc(cx.sem_ld, 16)
            cx.ld += 16

        @block.vector
        def _(vector):
            vector.wait_ge(cx.sem_ld, cx.ld)
            cx.vi(vector.tensor_copy(gw_sb[:], wld_g[:]))

        experts = [-1] + list(range(NE))
        tmark_h, smark_h, vmark_h = [], [], []
        vmark_cast13, tmark_y, vmark_yacc, vmark_cast2 = [], [], [], []
        for b in range(NB1):
            @block.sync
            def _(sync, b=b):
                if b > 0:
                    sync.wait_ge(cx.sem_st, cx.st)
                sync.dma_start(
                    out=xnT_blk.rearrange("p (s t) -> p s t", s=8),
                    in_=xn1T_d[:, :, b * TB1:(b + 1) * TB1].rearrange("s p t -> p s t"),
                ).then_inc(cx.sem_ld2, 16)
                sync.dma_start(
                    out=acc.rearrange("p (tt d) -> p tt d", tt=NT),
                    in_=x1_d[b * TB1:(b + 1) * TB1, :].rearrange(
                        "(tt p) d -> p tt d", p=128),
                ).then_inc(cx.sem_ld2, 16)
                cx.ld2 += 32

            @block.tensor
            def _(tensor, b=b):
                tensor.wait_ge(cx.sem_ld2, cx.ld2)
                tensor.wait_ge(cx.sem_v, cx.v)
                for tt in range(NT):
                    for ds in range(8):
                        cx.ti(tensor.matmul(
                            P_g[:, tt * NE:(tt + 1) * NE],
                            xnT_blk[:, ds * TB1 + tt * 128: ds * TB1 + (tt + 1) * 128],
                            gw_sb[:, ds * NE:(ds + 1) * NE],
                            start=(ds == 0), stop=(ds == 7)))

            @block.vector
            def _(vector, b=b):
                vector.wait_ge(cx.sem_t, cx.t)
                vector.tensor_copy(gl_all[:], P_g[:, 0:NT * NE])
                vector.tensor_copy(junk[:], junk2[:])
                vector.tensor_copy(junk2[:], junk[:])
                cx.vi(vector.tensor_copy(junk[:], junk2[:]))

            @block.scalar
            def _(scalar, b=b):
                scalar.wait_ge(cx.sem_v, cx.v)
                cx.si(scalar.activation(g_all[:], gl_all[:], AF.Sigmoid))

            @block.vector
            def _(vector, b=b):
                vector.wait_ge(cx.sem_s, cx.s)
                for tt in range(NT):
                    vector.reduce_max(m1L[:, tt:tt + 1],
                                      gl_all[:, tt * NE:(tt + 1) * NE], axis=AX.X)
                for tt in range(NT):
                    vector.tensor_scalar(msk_all[:, tt * NE:(tt + 1) * NE],
                                         gl_all[:, tt * NE:(tt + 1) * NE],
                                         m1L[:, tt:tt + 1], None, ALU.is_equal)
                for tt in range(NT):
                    vector.scalar_tensor_tensor(
                        mskd_all[:, tt * NE:(tt + 1) * NE],
                        msk_all[:, tt * NE:(tt + 1) * NE], -1e30,
                        gl_all[:, tt * NE:(tt + 1) * NE], ALU.mult, ALU.add)
                for tt in range(NT):
                    vector.reduce_max(m2L[:, tt:tt + 1],
                                      mskd_all[:, tt * NE:(tt + 1) * NE], axis=AX.X)
                for tt in range(NT):
                    vector.tensor_scalar(selt_all[:, tt * NE:(tt + 1) * NE],
                                         gl_all[:, tt * NE:(tt + 1) * NE],
                                         m2L[:, tt:tt + 1], None, ALU.is_ge)
                vector.tensor_copy(junk[:], junk2[:])
                vector.tensor_copy(junk2[:], junk[:])
                vector.tensor_copy(junk[:], junk2[:])
                vector.tensor_tensor(sg_all[:], selt_all[:], g_all[:], ALU.mult)
                vector.tensor_copy(junk[:], junk2[:])
                vector.tensor_copy(junk2[:], junk[:])
                vector.tensor_copy(junk[:], junk2[:])
                for tt in range(NT):
                    vector.reduce_sum(den8[:, tt:tt + 1],
                                      sg_all[:, tt * NE:(tt + 1) * NE], axis=AX.X)
                vector.tensor_copy(junk[:], junk2[:])
                vector.tensor_copy(junk2[:], junk[:])
                vector.tensor_copy(junk[:], junk2[:])
                vector.reciprocal(rd8[:], den8[:])
                vector.tensor_copy(junk[:], junk2[:])
                vector.tensor_copy(junk2[:], junk[:])
                vector.tensor_copy(junk[:], junk2[:])
                for tt in range(NT):
                    instr = vector.tensor_scalar(
                        wsel[:, tt * NE:(tt + 1) * NE],
                        sg_all[:, tt * NE:(tt + 1) * NE], rd8[:, tt:tt + 1],
                        1.0 / 3.0, ALU.mult, ALU.mult)
                cx.vi(instr)

            for ei, e in enumerate(experts):
                for q in range(NFS):
                    lbuf = q % 2
                    Q = len(vmark_cast13)

                    @block.sync
                    def _(sync, b=b, e=e, q=q, lbuf=lbuf, Q=Q):
                        if B2_SERIAL_H:
                            sync.wait_ge(cx.sem_v, cx.v)
                        elif Q >= 2:
                            sync.wait_ge(cx.sem_v, vmark_cast13[Q - 2])
                        s1, s3b = w13_src(e, q * FS, (q + 1) * FS)
                        sync.dma_start(
                            out=w13ld[lbuf][:, 0:8 * FS].rearrange(
                                "p (s f) -> p s f", s=8),
                            in_=s1.rearrange("(s p) f -> p s f", p=128),
                        ).then_inc(sem_w1[lbuf], 16)
                        c_w1[lbuf] += 16

                    @block.scalar
                    def _(scalar, b=b, e=e, q=q, lbuf=lbuf, Q=Q):
                        if B2_SERIAL_H:
                            scalar.wait_ge(cx.sem_v, cx.v)
                        elif Q >= 2:
                            scalar.wait_ge(cx.sem_v, vmark_cast13[Q - 2])
                        _s1, s3 = w13_src(e, q * FS, (q + 1) * FS)
                        scalar.dma_start(
                            out=w13ld[lbuf][:, 8 * FS:16 * FS].rearrange(
                                "p (s f) -> p s f", s=8),
                            in_=s3.rearrange("(s p) f -> p s f", p=128),
                        ).then_inc(sem_w3[lbuf], 16)
                        c_w3[lbuf] += 16

                    @block.vector
                    def _(vector, b=b, e=e, q=q, lbuf=lbuf, Q=Q,
                          w1c=None, w3c=None):
                        vector.wait_ge(sem_w1[lbuf], c_w1[lbuf])
                        vector.wait_ge(sem_w3[lbuf], c_w3[lbuf])
                        if B2_SERIAL_H:
                            vector.wait_ge(cx.sem_t, cx.t)
                        elif Q >= 2 and len(tmark_h) >= 5:
                            vector.wait_ge(cx.sem_t, tmark_h[4 * Q - 5])
                        vector.tensor_copy(wb13[lbuf][:], w13ld[lbuf][:])
                        vector.tensor_copy(junk[:], junk2[:])
                        cx.vi(vector.tensor_copy(junk2[:], junk[:]))
                        vmark_cast13.append(cx.v)

                    for ft in range(FS // 128):
                        for tc in range(TB1 // 512):
                            gg = len(tmark_h)
                            par = gg % 2

                            @block.tensor
                            def _(tensor, b=b, e=e, q=q, ft=ft, tc=tc,
                                  gg=gg, par=par, lbuf=lbuf):
                                if B2_SERIAL_H:
                                    tensor.wait_ge(cx.sem_v, cx.v)
                                    tensor.wait_ge(cx.sem_s, cx.s)
                                else:
                                    tensor.wait_ge(cx.sem_v, vmark_cast13[-1])
                                    if gg >= 2:
                                        tensor.wait_ge(cx.sem_s, smark_h[gg - 2])
                                        tensor.wait_ge(cx.sem_v, vmark_h[gg - 2])
                                for ds in range(8):
                                    cx.ti(tensor.matmul(
                                        P_h1[:, par * 512:par * 512 + 512],
                                        wb13[lbuf][:, ds * FS + ft * 128:
                                                   ds * FS + (ft + 1) * 128],
                                        xnT_blk[:, ds * TB1 + tc * 512:
                                                ds * TB1 + (tc + 1) * 512],
                                        start=(ds == 0), stop=(ds == 7)))
                                for ds in range(8):
                                    cx.ti(tensor.matmul(
                                        P_h3[:, par * 512:par * 512 + 512],
                                        wb13[lbuf][:, 8 * FS + ds * FS + ft * 128:
                                                   8 * FS + ds * FS + (ft + 1) * 128],
                                        xnT_blk[:, ds * TB1 + tc * 512:
                                                ds * TB1 + (tc + 1) * 512],
                                        start=(ds == 0), stop=(ds == 7)))
                                tmark_h.append(cx.t)

                            @block.scalar
                            def _(scalar, b=b, e=e, gg=gg, par=par):
                                scalar.wait_ge(cx.sem_t, cx.t)
                                if (not B2_SERIAL_H) and gg >= 2:
                                    scalar.wait_ge(cx.sem_v, vmark_h[gg - 2])
                                cx.si(scalar.activation(
                                    sl2[par][:], P_h1[:, par * 512:par * 512 + 512],
                                    AF.Silu))
                                smark_h.append(cx.s)

                            @block.vector
                            def _(vector, b=b, e=e, q=q, ft=ft, tc=tc,
                                  gg=gg, par=par):
                                vector.wait_ge(cx.sem_s, cx.s)
                                fslot = q * FS + ft * 128
                                cx.vi(vector.tensor_tensor(
                                    h_sb[:, (fslot // 128) * TB1 + tc * 512:
                                         (fslot // 128) * TB1 + (tc + 1) * 512],
                                    sl2[par][:], P_h3[:, par * 512:par * 512 + 512],
                                    ALU.mult))
                                vmark_h.append(cx.v)

                for fh in range(2):
                    for dc in range(2):
                        i = fh * 2 + dc
                        lbuf = i % 2
                        I = len(vmark_cast2)
                        eng = "sync" if i % 2 == 0 else "scalar"

                        @getattr(block, eng)
                        def _(engq, b=b, e=e, fh=fh, dc=dc, lbuf=lbuf, I=I):
                            if B2_SERIAL_Y:
                                engq.wait_ge(cx.sem_v, cx.v)
                            elif I >= 2:
                                engq.wait_ge(cx.sem_v, vmark_cast2[I - 2])
                            s2 = w2_src(e, fh * 1024, (fh + 1) * 1024)[:, dc * 512:(dc + 1) * 512]
                            engq.dma_start(
                                out=w2ld[lbuf].rearrange("p (s dd) -> p s dd", s=8),
                                in_=s2.rearrange("(s p) dd -> p s dd", p=128),
                            ).then_inc(sem_w2p[lbuf], 16)
                            c_w2[lbuf] += 16

                        @block.vector
                        def _(vector, b=b, e=e, lbuf=lbuf, I=I):
                            vector.wait_ge(sem_w2p[lbuf], c_w2[lbuf])
                            if B2_SERIAL_Y:
                                vector.wait_ge(cx.sem_t, cx.t)
                            elif I >= 2 and len(tmark_y) >= 9:
                                vector.wait_ge(cx.sem_t, tmark_y[8 * I - 9])
                            vector.tensor_copy(wb2[lbuf][:], w2ld[lbuf][:])
                            vector.tensor_copy(junk[:], junk2[:])
                            cx.vi(vector.tensor_copy(junk2[:], junk[:]))
                            vmark_cast2.append(cx.v)

                        for tt in range(NT):
                            yg = len(tmark_y)
                            ypar = yg % 2

                            @block.tensor
                            def _(tensor, b=b, e=e, fh=fh, dc=dc, tt=tt,
                                  yg=yg, ypar=ypar, lbuf=lbuf):
                                if B2_SERIAL_Y:
                                    tensor.wait_ge(cx.sem_v, cx.v)
                                else:
                                    tensor.wait_ge(cx.sem_v, vmark_cast2[-1])
                                    tensor.wait_ge(cx.sem_v, vmark_h[-1])
                                    if yg >= 2:
                                        tensor.wait_ge(cx.sem_v, vmark_yacc[yg - 2])
                                for fs in range(8):
                                    fglob = fh * 8 + fs
                                    cx.ti(tensor.matmul(
                                        P_y[:, ypar * 512:ypar * 512 + 512],
                                        h_sb[:, fglob * TB1 + tt * 128:
                                             fglob * TB1 + (tt + 1) * 128],
                                        wb2[lbuf][:, fs * 512:(fs + 1) * 512],
                                        start=(fs == 0), stop=(fs == 7)))
                                tmark_y.append(cx.t)

                            @block.vector
                            def _(vector, b=b, e=e, dc=dc, tt=tt, yg=yg, ypar=ypar):
                                vector.wait_ge(cx.sem_t, cx.t)
                                dst = acc[:, tt * D + dc * 512: tt * D + (dc + 1) * 512]
                                if e < 0:
                                    cx.vi(vector.scalar_tensor_tensor(
                                        dst, P_y[:, ypar * 512:ypar * 512 + 512],
                                        1.0 / 3.0, dst, ALU.mult, ALU.add))
                                else:
                                    cx.vi(vector.scalar_tensor_tensor(
                                        dst, P_y[:, ypar * 512:ypar * 512 + 512],
                                        wsel[:, tt * NE + e: tt * NE + e + 1],
                                        dst, ALU.mult, ALU.add))
                                vmark_yacc.append(cx.v)

            @block.gpsimd
            def _(gpsimd, b=b):
                gpsimd.wait_ge(cx.sem_v, cx.v)
                gpsimd.dma_start(
                    out=x2p_d[b * TB1:(b + 1) * TB1, :].rearrange(
                        "(tt p) d -> p tt d", p=128),
                    in_=acc.rearrange("p (tt d) -> p tt d", tt=NT),
                ).then_inc(cx.sem_st, 16)
                cx.st += 16

        @block.gpsimd
        def _(gpsimd):
            gpsimd.wait_ge(cx.sem_st, cx.st)
    return nc


B3_CUT = 99


def build_b3(fused_nc=None, store=None):
    nc = fused_nc or bass.Bass("TRN2", target_bir_lowering=False)
    ext = "ExternalInput"
    if store is None:
        qs_d = nc.dram_tensor("qs", [T_FULL, D], BF16, kind=ext).ap()
        kv_d = nc.dram_tensor("kv", [128, 512], F32, kind=ext).ap()
        ksum_d = nc.dram_tensor("ksum", [D], F32, kind=ext).ap()
        x2p_d = nc.dram_tensor("x2p", [T_FULL, D], F32, kind=ext).ap()
    else:
        qs_d = store["qs"]
        kv_d = store["kv"]
        ksum_d = store["ksum"]
        x2p_d = store["x2p"]
    wout_d = nc.dram_tensor("wout", [D, D], F32, kind=ext).ap()
    bout_d = nc.dram_tensor("bout_b", [128, D], F32, kind=ext).ap()
    idb_d = nc.dram_tensor("idb4", [128, 128], BF16, kind=ext).ap()
    onesr_d = nc.dram_tensor("onesr", [1, 128], BF16, kind=ext).ap()
    x2_d = nc.dram_tensor("x2", [T_FULL, D], F32, kind="ExternalOutput").ap()
    if store is not None:
        store["x2"] = x2_d

    cx = Ctx(nc) if fused_nc is None else store["cx"]
    _st0, _ld20 = cx.st, cx.ld2

    qs_sb = [sb(nc, "b3_qs0", [128, D], BF16), sb(nc, "b3_qs1", [128, D], BF16)]
    x2p_sb = [sb(nc, "b3_xp0", [128, D], F32), sb(nc, "b3_xp1", [128, D], F32)]
    kvbd_sb = sb(nc, "b3_kvbd", [128, 1024], BF16)
    kv_ld = sb(nc, "b3_kvld", [128, 512], F32)
    ksum_b = sb(nc, "b3_ksb", [128, D], F32)
    krow = sb(nc, "b3_krow", [1, D], F32)
    krow_bf = sb(nc, "b3_krowbf", [1, D], BF16)
    ones_r = sb(nc, "b3_onesr", [1, 128], BF16)
    wout_sb = sb(nc, "b3_wout", [128, 8 * D], BF16)
    wld = sb(nc, "b3_wld", [128, 8 * D], F32)
    bout_b = sb(nc, "b3_bout", [128, D], F32)
    ident_b = sb(nc, "b3_idb", [128, 128], BF16)
    tmpw = sb(nc, "b3_tmpw", [128, D], F32)
    nrm = sb(nc, "b3_nrm", [128, HEADS], F32)
    rn = sb(nc, "b3_rn", [128, HEADS], F32)
    junk = sb(nc, "b3_junk", [128, 1], F32)
    junk2 = sb(nc, "b3_junk2", [128, 1], F32)
    qsn_bf = sb(nc, "b3_qsn", [128, D], BF16)
    qsnT = sb(nc, "b3_qsnT", [128, D], BF16)
    o_bf = sb(nc, "b3_obf", [128, D], BF16)
    oT = sb(nc, "b3_oT", [128, D], BF16)
    x2_sb = [sb(nc, "b3_x20", [128, D], F32), sb(nc, "b3_x21", [128, D], F32)]

    P_o = ps(nc, "b3_Po", [128, 1024], F32)
    P_w = ps(nc, "b3_Pw", [128, 1024], F32)
    P_bf4 = ps(nc, "b3_Pbf", [128, 1024], BF16)

    with nc.Block() as block:
        if fused_nc is not None:
            cx.barrier(block)
        @block.sync
        def _(sync):
            sync.dma_start(out=ident_b[:], in_=idb_d).then_inc(cx.sem_ld, 16)
            sync.dma_start(out=bout_b[:], in_=bout_d).then_inc(cx.sem_ld, 16)
            sync.dma_start(out=kv_ld[:], in_=kv_d).then_inc(cx.sem_ld, 16)
            sync.dma_start(out=krow[:], in_=ksum_d.unsqueeze(0)
                           ).then_inc(cx.sem_ld, 16)
            sync.dma_start(out=ones_r[:], in_=onesr_d).then_inc(cx.sem_ld, 16)
            sync.dma_start(out=wld.rearrange("p (s n) -> p s n", s=8),
                           in_=wout_d.rearrange("(s p) n -> p s n", p=128)
                           ).then_inc(cx.sem_ld, 16)
            cx.ld += 96

        @block.vector
        def _(vector):
            vector.wait_ge(cx.sem_ld, cx.ld)
            vector.memset(kvbd_sb[:], 0.0)
            for dt in range(8):
                vector.tensor_copy(kvbd_sb[0:64, dt * 128:dt * 128 + 64],
                                   kv_ld[0:64, dt * 64:(dt + 1) * 64])
                vector.tensor_copy(kvbd_sb[64:128, dt * 128 + 64:dt * 128 + 128],
                                   kv_ld[64:128, dt * 64:(dt + 1) * 64])
            vector.tensor_copy(krow_bf[:], krow[:])
            cx.vi(vector.tensor_copy(wout_sb[:], wld[:]))

        @block.tensor
        def _(tensor):
            tensor.wait_ge(cx.sem_v, cx.v)
            for half in range(2):
                cx.ti(tensor.matmul(P_w[:, half * 512:(half + 1) * 512], ones_r[:],
                                    krow_bf[:, half * 512:(half + 1) * 512],
                                    start=True, stop=True))

        @block.vector
        def _(vector):
            vector.wait_ge(cx.sem_t, cx.t)
            cx.vi(vector.tensor_copy(ksum_b[:], P_w[:, 0:1024]))

        @block.sync
        def _(sync):
            sync.dma_start(out=qs_sb[0][:], in_=qs_d[0:128, :]).then_inc(cx.sem_ld2, 16)
            sync.dma_start(out=x2p_sb[0][:], in_=x2p_d[0:128, :]).then_inc(cx.sem_ld2, 16)
            cx.ld2 += 32

        for c in range(NCH):
            buf = c % 2
            if c + 1 < NCH:
                @block.sync
                def _(sync, c=c, buf=buf):
                    if c >= 1:
                        sync.wait_ge(cx.sem_st, _st0 + 16 * c)
                    sync.dma_start(out=qs_sb[1 - buf][:],
                                   in_=qs_d[(c + 1) * 128:(c + 2) * 128, :]
                                   ).then_inc(cx.sem_ld2, 16)
                    sync.dma_start(out=x2p_sb[1 - buf][:],
                                   in_=x2p_d[(c + 1) * 128:(c + 2) * 128, :]
                                   ).then_inc(cx.sem_ld2, 16)
                    cx.ld2 += 32

            @block.vector
            def _(vector, c=c, buf=buf):
                vector.wait_ge(cx.sem_ld2, _ld20 + 32 * (c + 1))
                if c > 0:
                    vector.wait_ge(cx.sem_t, cx.t)
                vector.tensor_tensor(tmpw[:], qs_sb[buf][:], ksum_b[:], ALU.mult)
                vector.tensor_copy(junk[:], junk2[:])
                vector.tensor_copy(junk2[:], junk[:])
                vector.tensor_copy(junk[:], junk2[:])
                for h in range(HEADS):
                    vector.reduce_sum(nrm[:, h:h + 1],
                                      tmpw[:, h * DH:(h + 1) * DH], axis=AX.X)
                vector.tensor_copy(junk[:], junk2[:])
                vector.tensor_copy(junk2[:], junk[:])
                vector.tensor_copy(junk[:], junk2[:])
                vector.tensor_scalar_add(nrm[:], nrm[:], EPS_DEN)
                vector.tensor_copy(junk[:], junk2[:])
                vector.tensor_copy(junk2[:], junk[:])
                vector.tensor_copy(junk[:], junk2[:])
                vector.reciprocal(rn[:], nrm[:])
                vector.tensor_copy(junk[:], junk2[:])
                vector.tensor_copy(junk2[:], junk[:])
                vector.tensor_copy(junk[:], junk2[:])
                for h in range(HEADS):
                    vector.tensor_scalar_mul(
                        qsn_bf[:, h * DH:(h + 1) * DH],
                        qs_sb[buf][:, h * DH:(h + 1) * DH], rn[:, h:h + 1])
                vector.tensor_copy(junk[:], junk2[:])
                cx.vi(vector.tensor_copy(junk2[:], junk[:]))

            if B3_CUT <= 1:
                @block.gpsimd
                def _(gpsimd, c=c, buf=buf):
                    gpsimd.wait_ge(cx.sem_v, cx.v)
                    gpsimd.dma_start(out=x2_d[c * 128:(c + 1) * 128, :],
                                     in_=x2p_sb[buf][:]).then_inc(cx.sem_st, 16)
                    cx.st += 16
                continue

            @block.tensor
            def _(tensor, c=c):
                tensor.wait_ge(cx.sem_v, cx.v)
                for ds in range(8):
                    cx.ti(tensor.transpose(
                        P_bf4[:, ds * 128:(ds + 1) * 128],
                        qsn_bf[:, ds * 128:(ds + 1) * 128], ident_b[:]))

            @block.vector
            def _(vector, c=c):
                vector.wait_ge(cx.sem_t, cx.t)
                cx.vi(vector.tensor_copy(qsnT[:], P_bf4[:, 0:1024]))

            if B3_CUT <= 2:
                @block.gpsimd
                def _(gpsimd, c=c, buf=buf):
                    gpsimd.wait_ge(cx.sem_v, cx.v)
                    gpsimd.dma_start(out=x2_d[c * 128:(c + 1) * 128, :],
                                     in_=x2p_sb[buf][:]).then_inc(cx.sem_st, 16)
                    cx.st += 16
                continue

            @block.tensor
            def _(tensor, c=c):
                tensor.wait_ge(cx.sem_v, cx.v)
                for dt in range(8):
                    cx.ti(tensor.matmul(
                        P_o[:, dt * 128:(dt + 1) * 128],
                        qsnT[:, dt * 128:(dt + 1) * 128],
                        kvbd_sb[:, dt * 128:(dt + 1) * 128],
                        start=True, stop=True))

            @block.vector
            def _(vector, c=c):
                vector.wait_ge(cx.sem_t, cx.t)
                cx.vi(vector.tensor_copy(o_bf[:], P_o[:, 0:1024]))

            if B3_CUT <= 3:
                @block.gpsimd
                def _(gpsimd, c=c, buf=buf):
                    gpsimd.wait_ge(cx.sem_v, cx.v)
                    gpsimd.dma_start(out=x2_d[c * 128:(c + 1) * 128, :],
                                     in_=x2p_sb[buf][:]).then_inc(cx.sem_st, 16)
                    cx.st += 16
                continue

            @block.tensor
            def _(tensor, c=c):
                tensor.wait_ge(cx.sem_v, cx.v)
                for ds in range(8):
                    cx.ti(tensor.transpose(
                        P_bf4[:, ds * 128:(ds + 1) * 128],
                        o_bf[:, ds * 128:(ds + 1) * 128], ident_b[:]))

            @block.vector
            def _(vector, c=c):
                vector.wait_ge(cx.sem_t, cx.t)
                cx.vi(vector.tensor_copy(oT[:], P_bf4[:, 0:1024]))

            @block.tensor
            def _(tensor, c=c):
                tensor.wait_ge(cx.sem_v, cx.v)
                for half in range(2):
                    for ds in range(8):
                        cx.ti(tensor.matmul(
                            P_w[:, half * 512:(half + 1) * 512],
                            oT[:, ds * 128:(ds + 1) * 128],
                            wout_sb[:, ds * D + half * 512: ds * D + (half + 1) * 512],
                            start=(ds == 0), stop=(ds == 7)))

            @block.vector
            def _(vector, c=c, buf=buf):
                vector.wait_ge(cx.sem_t, cx.t)
                if c >= 2:
                    vector.wait_ge(cx.sem_st, _st0 + 16 * (c - 1))
                vector.tensor_tensor(tmpw[:], P_w[:, 0:1024], bout_b[:], ALU.add)
                cx.vi(vector.tensor_tensor(x2_sb[buf][:], tmpw[:], x2p_sb[buf][:],
                                           ALU.add))

            @block.gpsimd
            def _(gpsimd, c=c, buf=buf):
                gpsimd.wait_ge(cx.sem_v, cx.v)
                gpsimd.dma_start(out=x2_d[c * 128:(c + 1) * 128, :],
                                 in_=x2_sb[buf][:]).then_inc(cx.sem_st, 16)
                cx.st += 16

        @block.gpsimd
        def _(gpsimd):
            gpsimd.wait_ge(cx.sem_st, cx.st)
    return nc


def _consts():
    return {
        "idf": np.eye(128, dtype=np.float32),
        "idb": np.eye(128, dtype=np.float32).astype(ml_dtypes.bfloat16),
        "msl": np.tril(np.ones((128, 128), np.float32), -1),
        "msu": np.triu(np.ones((128, 128), np.float32), 1),
        "mui": np.triu(np.ones((128, 128), np.float32), 0),
    }


def _a1_maps(inputs):
    cn = _consts()
    maps = []
    for b in range(8):
        maps.append(dict(
            x=np.ascontiguousarray(inputs["L0_attn_x"][b]),
            wqT=np.ascontiguousarray(inputs["L0_attn_Wq"].T),
            wkT=np.ascontiguousarray(inputs["L0_attn_Wk"].T),
            wvT=np.ascontiguousarray(inputs["L0_attn_Wv"].T),
            woT=np.ascontiguousarray(inputs["L0_attn_Wo"].T),
            bq_b=np.ascontiguousarray(np.broadcast_to(inputs["L0_attn_bq"], (128, H))),
            bk_b=np.ascontiguousarray(np.broadcast_to(inputs["L0_attn_bk"], (128, H))),
            bv_b=np.ascontiguousarray(np.broadcast_to(inputs["L0_attn_bv"], (128, H))),
            bo_b=np.ascontiguousarray(np.broadcast_to(inputs["L0_attn_bo"], (128, D))),
            **cn,
        ))
    return maps


def _a2_maps(inputs, prev):
    b1l = np.ascontiguousarray(
        inputs["L0_ffn_b1"].reshape(NE, 4, 128).transpose(2, 0, 1).reshape(128, NE * 4))
    maps = []
    for b in range(8):
        maps.append(dict(
            xnT=prev[b]["xnT"],
            x1p=prev[b]["x1p"],
            gw=inputs["L0_ffn_gate_w"],
            w1=inputs["L0_ffn_w1"],
            b1l=b1l,
            w2=inputs["L0_ffn_w2"],
            b2l=inputs["L0_ffn_b2"],
            idb2=_consts()["idb"],
        ))
    return maps


def _b1_maps(inputs, prev):
    cn = _consts()
    maps = []
    for b in range(8):
        maps.append(dict(
            x1=prev[b]["x1"],
            wqkv=inputs["L1_attn_W_qkv"],
            qnw_b=np.ascontiguousarray(np.broadcast_to(inputs["L1_attn_qn_w"], (128, D))),
            knw_b=np.ascontiguousarray(np.broadcast_to(inputs["L1_attn_kn_w"], (128, D))),
            idb3=cn["idb"],
            ones_c=np.ones((128, 1), dtype=ml_dtypes.bfloat16),
        ))
    return maps


def _b2_maps(inputs, prev):
    maps = []
    for b in range(8):
        maps.append(dict(
            xn1T=prev[b]["xn1T"],
            x1=prev[b]["x1"],
            wg=inputs["L1_ffn_Wg"],
            ws1=inputs["L1_ffn_Ws1"],
            ws3=inputs["L1_ffn_Ws3"],
            ws2=inputs["L1_ffn_Ws2"],
            we1=inputs["L1_ffn_We1"],
            we3=inputs["L1_ffn_We3"],
            we2=inputs["L1_ffn_We2"],
        ))
    return maps


def _b3_maps(inputs, prev):
    cn = _consts()
    maps = []
    for b in range(8):
        maps.append(dict(
            qs=prev[b]["qs"],
            kv=prev[b]["kv"],
            ksum=prev[b]["ksum"],
            x2p=prev[b]["x2p"],
            wout=inputs["L1_attn_W_out"],
            bout_b=np.ascontiguousarray(np.broadcast_to(inputs["L1_attn_b_out"], (128, D))),
            idb4=cn["idb"],
            onesr=np.ones((1, 128), dtype=ml_dtypes.bfloat16),
        ))
    return maps


def run_a1(inputs):
    nc = build_a1()
    res = run_bass_kernel_spmd(nc, _a1_maps(inputs), core_ids=list(range(8)))
    return res.results


def kernel_unfused(**inputs):
    r1 = run_a1(inputs)
    nc2 = build_a2()
    r2 = run_bass_kernel_spmd(nc2, _a2_maps(inputs, r1), core_ids=list(range(8))).results
    nc3 = build_b1()
    r3 = run_bass_kernel_spmd(nc3, _b1_maps(inputs, r2), core_ids=list(range(8))).results
    merged = [dict(r2[b], **r3[b]) for b in range(8)]
    nc4 = build_b2()
    r4 = run_bass_kernel_spmd(nc4, _b2_maps(inputs, merged), core_ids=list(range(8))).results
    merged2 = [dict(merged[b], **r4[b]) for b in range(8)]
    nc5 = build_b3()
    r5 = run_bass_kernel_spmd(nc5, _b3_maps(inputs, merged2), core_ids=list(range(8))).results
    out = np.stack([r5[b]["x2"] for b in range(8)], axis=0).astype(np.float32)
    return out


def _fused_maps(inputs):
    cn = _consts()
    b1l = np.ascontiguousarray(
        inputs["L0_ffn_b1"].reshape(NE, 4, 128).transpose(2, 0, 1).reshape(128, NE * 4))
    maps = []
    for b in range(8):
        maps.append(dict(
            x=np.ascontiguousarray(inputs["L0_attn_x"][b]),
            wqT=np.ascontiguousarray(inputs["L0_attn_Wq"].T),
            wkT=np.ascontiguousarray(inputs["L0_attn_Wk"].T),
            wvT=np.ascontiguousarray(inputs["L0_attn_Wv"].T),
            woT=np.ascontiguousarray(inputs["L0_attn_Wo"].T),
            bq_b=np.ascontiguousarray(np.broadcast_to(inputs["L0_attn_bq"], (128, H))),
            bk_b=np.ascontiguousarray(np.broadcast_to(inputs["L0_attn_bk"], (128, H))),
            bv_b=np.ascontiguousarray(np.broadcast_to(inputs["L0_attn_bv"], (128, H))),
            bo_b=np.ascontiguousarray(np.broadcast_to(inputs["L0_attn_bo"], (128, D))),
            gw=inputs["L0_ffn_gate_w"],
            w1=inputs["L0_ffn_w1"],
            b1l=b1l,
            w2=inputs["L0_ffn_w2"],
            b2l=inputs["L0_ffn_b2"],
            idb2=cn["idb"],
            wqkv=inputs["L1_attn_W_qkv"],
            qnw_b=np.ascontiguousarray(np.broadcast_to(inputs["L1_attn_qn_w"], (128, D))),
            knw_b=np.ascontiguousarray(np.broadcast_to(inputs["L1_attn_kn_w"], (128, D))),
            idb3=cn["idb"],
            ones_c=np.ones((128, 1), dtype=ml_dtypes.bfloat16),
            wg=inputs["L1_ffn_Wg"],
            ws1=inputs["L1_ffn_Ws1"],
            ws3=inputs["L1_ffn_Ws3"],
            ws2=inputs["L1_ffn_Ws2"],
            we1=inputs["L1_ffn_We1"],
            we3=inputs["L1_ffn_We3"],
            we2=inputs["L1_ffn_We2"],
            wout=inputs["L1_attn_W_out"],
            bout_b=np.ascontiguousarray(np.broadcast_to(inputs["L1_attn_b_out"], (128, D))),
            idb4=cn["idb"],
            onesr=np.ones((1, 128), dtype=ml_dtypes.bfloat16),
            **cn,
        ))
    return maps


def build_fused():
    global _ARENA
    nc = bass.Bass("TRN2", target_bir_lowering=False)
    store = {}
    for build in (build_a1, build_a2, build_b1, build_b2, build_b3):
        _ARENA = contextlib.ExitStack()
        build(fused_nc=nc, store=store)
        _ARENA.close()
    _ARENA = None
    return nc


def kernel(**inputs):
    nc = build_fused()
    res = run_bass_kernel_spmd(nc, _fused_maps(inputs), core_ids=list(range(8)))
    out = np.stack([res.results[b]["x2"] for b in range(8)], axis=0).astype(np.float32)
    return out
